# version 08
# baseline (speedup 1.0000x reference)
import math
import numpy as np
import ml_dtypes
from contextlib import ExitStack

import concourse.bass as bass
import concourse.mybir as mybir
from concourse.bass_utils import run_bass_kernel_spmd

F32 = mybir.dt.float32
BF16 = mybir.dt.bfloat16
I32 = mybir.dt.int32
U8 = mybir.dt.uint8
AF = mybir.ActivationFunctionType
ALU = mybir.AluOpType
AX = mybir.AxisListType

D = 4096
T = 2048
KC = 32
NT = 16
NSEG = 4
EPS = 1e-5
SQRT512 = math.sqrt(512.0)
RSQRT128 = 1.0 / math.sqrt(128.0)


def fourier_basis_f32():
    n = D
    t = np.arange(n)
    B = np.empty((n, n), np.float64)
    B[0] = 1.0 / math.sqrt(n)
    f = np.arange(1, n // 2)
    ang = 2.0 * np.pi * np.outer(f, t) / n
    B[2 * f - 1] = math.sqrt(2.0 / n) * np.cos(ang)
    B[2 * f] = math.sqrt(2.0 / n) * np.sin(ang)
    B[n - 1] = (1.0 / math.sqrt(n)) * ((-1.0) ** t)
    return np.ascontiguousarray(B.astype(np.float32))


def bcast_inner(ap, inner):
    return bass.AP(ap.tensor, ap.offset, list(ap.ap) + [[0, inner]])


class Ctx:
    def __init__(self, nc, S, block):
        self.nc, self.S, self.block = nc, S, block
        self._sems = {}
        self.n = {}
        for name in ("pe", "dv", "ac", "gp", "hd", "gd", "cc"):
            self.sem(name)

    def sem(self, name):
        if name not in self._sems:
            self._sems[name] = self.S.enter_context(self.nc.semaphore(name))
            self.n[name] = 0
        return self._sems[name]

    def bump(self, name, k=1):
        self.sem(name)
        self.n[name] += k
        return self.n[name]

    def at(self, name):
        self.sem(name)
        return self.n[name]

    def sb(self, name, shape, dtype):
        return self.S.enter_context(self.nc.sbuf_tensor(name, shape, dtype))

    def ps(self, name, shape, dtype=F32):
        return self.S.enter_context(self.nc.psum_tensor(name, shape, dtype))


def wait_tag(eng, c, tag):
    eng.wait_ge(c.sem(tag[0]), tag[1])


def emit_smalls(c):
    block = c.block
    sm = {}
    sm["identb"] = c.sb("identb", [128, 128], BF16)
    sm["identf"] = c.sb("identf", [128, 128], F32)
    colidx = c.sb("colidx", [128, 128], F32)
    partidx = c.sb("partidx", [128, 1], F32)
    sm["onescol"] = c.sb("onescol", [128, 1], BF16)
    sm["onesrow"] = c.sb("onesrow", [1, 128], F32)
    sm["eps"] = c.sb("eps", [128, 1], F32)
    sm["neg60"] = c.sb("neg60", [128, 1], F32)
    gp, dv = c.sem("gp"), c.sem("dv")

    def a0(scalar):
        from concourse.hw_specs import get_activation_tables
        for idx, (nm, funcs) in enumerate(
                get_activation_tables(c.nc.m.arch).items()):
            if AF.Exp in funcs and AF.Ln in funcs:
                scalar.add_instruction(mybir.InstLoadActFuncSet(
                    name=c.nc.get_next_instruction_name(),
                    act_func_set_id=idx, ins=[], outs=[]))
                break
    block.scalar(a0)

    def g(gpsimd):
        gpsimd.iota(colidx[:], base=0, pattern=[[1, 128]], channel_multiplier=0,
                    allow_small_or_imprecise_dtypes=True)
        gpsimd.iota(partidx[:], base=0, pattern=[[0, 1]], channel_multiplier=1,
                    allow_small_or_imprecise_dtypes=True)
        gpsimd.memset(sm["onescol"][:], 1.0)
        gpsimd.memset(sm["onesrow"][:], 1.0)
        gpsimd.memset(sm["eps"][:], EPS)
        gpsimd.memset(sm["neg60"][:], -60.0).then_inc(gp, 1)
        c.bump("gp")
    block.gpsimd(g)
    gp_t = c.at("gp")

    def v(vector):
        vector.wait_ge(gp, gp_t)
        vector.tensor_scalar(out=sm["identb"][:], in0=colidx[:],
                             scalar1=partidx[:, 0:1], scalar2=None,
                             op0=ALU.is_equal)
        vector.tensor_scalar(out=sm["identf"][:], in0=colidx[:],
                             scalar1=partidx[:, 0:1], scalar2=None,
                             op0=ALU.is_equal).then_inc(dv, 1)
        c.bump("dv")
    block.vector(v)
    sm["ready"] = ("dv", c.at("dv"))
    return sm


def emit_bias_T(c, sm, row_dram, nchunks, name, ready_tags=()):
    block = c.block
    colT = c.sb(name + "T", [128, nchunks], F32)
    _Sp = ExitStack()
    row = _Sp.enter_context(c.nc.sbuf_tensor(name + "_row",
                                             [1, nchunks * 128], F32))
    pt = _Sp.enter_context(c.nc.psum_tensor(name + "_ps", [128, nchunks], F32))
    dn = name + "_d"
    hd, pe, dv = c.sem(dn), c.sem("pe"), c.sem("dv")

    def s(sync):
        sync.wait_ge(pe, c.at("pe"))
        for rt in ready_tags:
            wait_tag(sync, c, rt)
        sync.dma_start(out=row[:], in_=row_dram[:, :]).then_inc(hd, 16)
        c.bump(dn, 16)
    block.sync(s)
    hd_t = c.at(dn)

    def t(tensor):
        tensor.wait_ge(hd, hd_t)
        wait_tag(tensor, c, sm["ready"])
        tensor.wait_ge(dv, c.at("dv"))
        last = None
        for j in range(nchunks):
            last = tensor.transpose(pt[:, j:j + 1],
                                    row[0:1, j * 128:(j + 1) * 128],
                                    sm["identf"][0:1, 0:1])
        last.then_inc(pe, 1)
        c.bump("pe")
    block.tensor(t)
    pe_t = c.at("pe")

    def v(vector):
        vector.wait_ge(pe, pe_t)
        vector.tensor_copy(colT[:], pt[:]).then_inc(dv, 1)
        c.bump("dv")
    block.vector(v)
    _Sp.close()
    return colT, ("dv", c.at("dv"))


def emit_load_stream(c, h, src_dram, cast, name="sh"):
    block = c.block
    srcr = src_dram.rearrange("(kc p) t -> p kc t", p=128)
    sem = c.sem(name)

    def g(eng):
        for i in range(4):
            sl = slice(i * KC // 4, (i + 1) * KC // 4)
            eng.dma_start(out=h[:, sl, :], in_=srcr[:, sl, :]).then_inc(sem, 16)
            c.bump(name, 16)
    (block.gpsimd if cast else block.sync)(g)
    return (name, c.at(name))



def emit_dense(c, *, WT_dram, kchunks, ochunks, moving, sink, extra_mm=None,
               tag="dn", ready_tags=()):
    block = c.block
    kh = kchunks // 4 if kchunks % 4 == 0 else (
        kchunks // 2 if kchunks % 2 == 0 else kchunks)
    nhalf = kchunks // kh
    wf32 = [c.sb(f"{tag}_f{i}", [128, kh, 128], F32) for i in range(min(nhalf, 2))]
    wcol = [c.sb(f"{tag}_w{i}", [128, kchunks, 128], BF16) for i in range(2)]
    NP = 4
    pacc = [c.ps(f"{tag}_p{i}", [128, 512], F32) for i in range(NP)]
    pe, ac = c.sem("pe"), c.sem("ac")
    fsem = [c.sem(f"{tag}_f{i}d") for i in range(min(nhalf, 2))]

    WTr = WT_dram.rearrange("(kc p) o -> p kc o", p=128)
    mm_done = {}
    conv_done = {}
    consumed = {}
    pending = []

    for oc in range(ochunks):
        b = oc % 2

        for half in range(nhalf):
            hsl = slice(half * kh, (half + 1) * kh)

            def s_w(sync):
                if oc < 2 and half == 0:
                    for rt in ready_tags:
                        wait_tag(sync, c, rt)
                hb = half % 2
                gk = oc * nhalf + half
                key = gk - 2
                if key in conv_done:
                    sync.wait_ge(ac, conv_done[key])
                sync.dma_start(
                    out=wf32[hb][:],
                    in_=WTr[:, hsl, oc * 128:(oc + 1) * 128]).then_inc(
                        fsem[hb], 16)
                c.bump(f"{tag}_f{hb}d", 16)
            block.sync(s_w)
            f_t = c.at(f"{tag}_f{half % 2}d")

            def a_w(scalar):
                scalar.wait_ge(fsem[half % 2], f_t)
                if oc >= 2 and half == 0:
                    scalar.wait_ge(pe, mm_done[oc - 2])
                scalar.copy(wcol[b][:, hsl, :],
                            wf32[half % 2][:]).then_inc(ac, 1)
                conv_done[oc * nhalf + half] = c.bump("ac")
            block.scalar(a_w)
        cv_t = conv_done[oc * nhalf + nhalf - 1]

        for tseg in range(NSEG):
            uses = oc * NSEG + tseg
            pidx = uses % NP

            def t(tensor):
                if tseg == 0:
                    tensor.wait_ge(ac, cv_t)
                if uses >= NP:
                    wait_tag(tensor, c, consumed[uses - NP])
                last = None
                for dc in range(kchunks):
                    last = tensor.matmul(
                        pacc[pidx][:, :],
                        wcol[b][:, dc, :],
                        moving[:, dc, tseg * 512:(tseg + 1) * 512],
                        start=(dc == 0), stop=(dc == kchunks - 1
                                               and extra_mm is None),
                        skip_group_check=True)
                if extra_mm is not None:
                    last = extra_mm(tensor, oc, tseg, pacc[pidx])
                last.then_inc(pe, 1)
                mm_done[oc] = c.bump("pe")
            block.tensor(t)

            if pending:
                o2, s2 = pending.pop(0)
                sink.post(c, o2, s2)
                if s2 == NSEG - 1:
                    sink.oc_done(c, o2)
            consumed[uses] = sink.evac(c, oc, tseg, pacc[pidx])
            pending.append((oc, tseg))
    while pending:
        o2, s2 = pending.pop(0)
        sink.post(c, o2, s2)
        if s2 == NSEG - 1:
            sink.oc_done(c, o2)
    sink.finish(c)


class BaseSink:
    def post(self, c, oc, tseg):
        pass

    def oc_done(self, c, oc):
        pass

    def finish(self, c):
        pass



class LNSink(BaseSink):
    def __init__(self, c, sm, tag, resid, bias_colT, s_scr, psstat, mask=None):
        self.sm, self.tag = sm, tag
        self.resid = resid
        self.bias_colT = bias_colT
        self.mask = mask
        self.schunk = c.sb(tag + "_sc", [128, 2, T], BF16)
        self.sq = c.sb(tag + "_sq", [128, 2, 512], BF16)
        if mask is not None:
            self.mtmp = c.sb(tag + "_mt", [128, 512], BF16)
        self.psstat = psstat
        self.first = True
        self.scrr = s_scr.rearrange("(kc p) t -> p kc t", p=128)
        self.ssem = [c.sem(tag + "_s0d"), c.sem(tag + "_s1d")]
        self.snames = [tag + "_s0d", tag + "_s1d"]
        self.evac_cnt = {}
        self.sq_cnt = {}
        self.stat_cnt = {}
        self.store_cnt = {}

    def evac(self, c, oc, tseg, psum):
        block, sm = c.block, self.sm
        pe, dv, ac = (c.sem(x) for x in ("pe", "dv", "ac"))
        pe_t = c.at("pe")
        sb = self.schunk[:, oc % 2, :]
        seg = slice(tseg * 512, (tseg + 1) * 512)

        def v(vector):
            vector.wait_ge(pe, pe_t)
            if oc >= 2:
                if tseg == 0 and (oc - 2) in self.store_cnt:
                    vector.wait_ge(self.ssem[oc % 2], self.store_cnt[oc - 2])
                if (oc - 2, tseg) in self.stat_cnt:
                    vector.wait_ge(pe, self.stat_cnt[(oc - 2, tseg)])
            if self.mask is not None:
                rcolT, qB = self.mask
                vector.wait_ge(dv, c.at("dv"))
                vector.tensor_scalar(
                    out=self.mtmp[:], in0=psum[:, :],
                    scalar1=self.bias_colT[:, oc:oc + 1],
                    scalar2=rcolT[:, oc:oc + 1],
                    op0=ALU.add, op1=ALU.mult).then_inc(dv, 1)
                c.bump("dv")
                vector.wait_ge(dv, c.at("dv"))
                vector.scalar_tensor_tensor(
                    out=self.mtmp[:], in0=self.mtmp[:], scalar=1.0,
                    in1=qB[:, seg], op0=ALU.mult, op1=ALU.mult).then_inc(dv, 1)
                c.bump("dv")
                vector.wait_ge(dv, c.at("dv"))
                vector.tensor_tensor(
                    out=sb[:, seg], in0=self.mtmp[:],
                    in1=self.resid[:, oc, seg], op=ALU.add).then_inc(dv, 1)
                c.bump("dv")
            elif self.bias_colT is not None and self.resid is not None:
                vector.scalar_tensor_tensor(
                    out=sb[:, seg], in0=psum[:, :],
                    scalar=self.bias_colT[:, oc:oc + 1],
                    in1=self.resid[:, oc, seg],
                    op0=ALU.add, op1=ALU.add).then_inc(dv, 1)
                c.bump("dv")
            elif self.bias_colT is not None:
                vector.tensor_scalar(
                    out=sb[:, seg], in0=psum[:, :],
                    scalar1=self.bias_colT[:, oc:oc + 1], scalar2=None,
                    op0=ALU.add).then_inc(dv, 1)
                c.bump("dv")
            else:
                vector.tensor_tensor(
                    out=sb[:, seg], in0=psum[:, :],
                    in1=self.resid[:, oc, seg], op=ALU.add).then_inc(dv, 1)
                c.bump("dv")
        block.vector(v)
        self.evac_cnt[(oc, tseg)] = c.at("dv")
        dv_t = c.at("dv")
        psum_tag = ("dv", dv_t if self.mask is None else dv_t - 1)

        def a(scalar):
            scalar.wait_ge(dv, dv_t)
            prev = (oc, tseg - 1) if tseg > 0 else (oc - 2, NSEG - 1)
            if prev in self.stat_cnt:
                scalar.wait_ge(pe, self.stat_cnt[prev])
            scalar.activation(out=self.sq[:, oc % 2, :], in_=sb[:, seg],
                              func=AF.Square).then_inc(ac, 1)
            c.bump("ac")
        block.scalar(a)
        self.sq_cnt[(oc, tseg)] = c.at("ac")
        return psum_tag

    def post(self, c, oc, tseg):
        block, sm = c.block, self.sm
        pe, dv, ac = c.sem("pe"), c.sem("dv"), c.sem("ac")
        sb = self.schunk[:, oc % 2, :]
        dv_t = self.evac_cnt[(oc, tseg)]
        ac_t = self.sq_cnt[(oc, tseg)]

        def t(tensor):
            tensor.wait_ge(dv, dv_t)
            tensor.wait_ge(ac, ac_t)
            last = None
            for tb in range(4):
                tt = tseg * 4 + tb
                blk = slice(tseg * 512 + tb * 128, tseg * 512 + (tb + 1) * 128)
                tensor.matmul(self.psstat[:, tt:tt + 1], sb[:, blk],
                              sm["onescol"][:, :],
                              start=self.first, stop=False,
                              skip_group_check=True)
                self.first = False
                last = tensor.matmul(self.psstat[:, 16 + tt:17 + tt],
                                     self.sq[:, oc % 2,
                                             tb * 128:(tb + 1) * 128],
                                     sm["onescol"][:, :],
                                     start=False, stop=False,
                                     skip_group_check=True)
            last.then_inc(pe, 1)
            self.stat_cnt[(oc, tseg)] = c.bump("pe")
        block.tensor(t)

    def oc_done(self, c, oc):
        block = c.block
        dv = c.sem("dv")
        dv_t = self.evac_cnt[(oc, NSEG - 1)]

        def s(sync):
            sync.wait_ge(dv, dv_t)
            sync.dma_start(out=self.scrr[:, oc, :],
                           in_=self.schunk[:, oc % 2, :]).then_inc(
                               self.ssem[oc % 2], 16)
            self.store_cnt[oc] = c.bump(self.snames[oc % 2], 16)
        block.sync(s)

    def store_tags(self, c):
        return [(n, c.at(n)) for n in self.snames]



def emit_normalize(c, sm, tag, s_scr, h, psstat, flat_scr, dest_mode="h",
                   out_dram=None, store_tags=()):
    block = c.block
    pe, dv, ac = (c.sem(x) for x in ("pe", "dv", "ac"))
    fsem = c.sem(tag + "_fd")
    lsem = [c.sem(tag + "_l0d"), c.sem(tag + "_l1d")]
    lnames = [tag + "_l0d", tag + "_l1d"]
    osem = c.sem(tag + "_od")

    mean = c.sb(tag + "_mean", [128, 16], F32)
    var = c.sb(tag + "_var", [128, 16], F32)
    rstd = c.sb(tag + "_rstd", [128, 16], F32)
    bneg = c.sb(tag + "_bneg", [128, 16], F32)
    tmp16 = c.sb(tag + "_tmp16", [128, 16], F32)
    rows_sb = c.sb(tag + "_rows", [16, 256], F32)
    rows2 = c.sb(tag + "_rows2", [1, 2 * T], F32)
    aB = c.sb(tag + "_aB", [128, T], BF16)
    bB = c.sb(tag + "_bB", [128, T], BF16)
    ptr = c.ps(tag + "_ptr", [16, 256], F32)
    pbb = c.ps(tag + "_pbb", [128, 512], F32)

    last_stat = c.at("pe")

    def v1(vector):
        vector.wait_ge(pe, last_stat)
        for t_ in store_tags:
            wait_tag(vector, c, t_)
        vector.tensor_scalar(out=mean[:], in0=psstat[:, 0:16],
                             scalar1=1.0 / D, scalar2=None, op0=ALU.mult)
        vector.tensor_scalar(out=var[:], in0=psstat[:, 16:32],
                             scalar1=1.0 / D, scalar2=None,
                             op0=ALU.mult).then_inc(dv, 1)
        c.bump("dv")
        vector.wait_ge(dv, c.at("dv"))
        vector.tensor_tensor(out=tmp16[:], in0=mean[:], in1=mean[:],
                             op=ALU.mult).then_inc(dv, 1)
        c.bump("dv")
        vector.wait_ge(dv, c.at("dv"))
        vector.tensor_tensor(out=var[:], in0=var[:], in1=tmp16[:],
                             op=ALU.subtract).then_inc(dv, 1)
        c.bump("dv")
    block.vector(v1)
    dv_t = c.at("dv")

    def a1(scalar):
        scalar.wait_ge(dv, dv_t)
        scalar.activation(out=var[:], in_=var[:], func=AF.Ln,
                          bias=sm["eps"][:, 0:1], scale=1.0).then_inc(ac, 1)
        c.bump("ac")
        scalar.wait_ge(ac, c.at("ac"))
        scalar.activation(out=rstd[:], in_=var[:], func=AF.Exp,
                          scale=-0.5).then_inc(ac, 1)
        c.bump("ac")
    block.scalar(a1)
    ac_t = c.at("ac")

    def v2(vector):
        vector.wait_ge(ac, ac_t)
        vector.tensor_scalar(out=tmp16[:], in0=mean[:], scalar1=-1.0,
                             scalar2=None, op0=ALU.mult).then_inc(dv, 1)
        c.bump("dv")
        vector.wait_ge(dv, c.at("dv"))
        vector.tensor_tensor(out=bneg[:], in0=tmp16[:], in1=rstd[:],
                             op=ALU.mult).then_inc(dv, 1)
        c.bump("dv")
    block.vector(v2)
    dv_t2 = c.at("dv")

    def t1(tensor):
        tensor.wait_ge(dv, dv_t2)
        tensor.transpose(ptr[0:16, 0:128], rstd[:], sm["identf"][:])
        tensor.transpose(ptr[0:16, 128:256], bneg[:],
                         sm["identf"][:]).then_inc(pe, 1)
        c.bump("pe")
    block.tensor(t1)
    pe_t = c.at("pe")

    def v3(vector):
        vector.wait_ge(pe, pe_t)
        vector.tensor_copy(rows_sb[:, :], ptr[0:16, :]).then_inc(dv, 1)
        c.bump("dv")
    block.vector(v3)
    dv_t3 = c.at("dv")

    def g1(gpsimd):
        gpsimd.wait_ge(dv, dv_t3)
        gpsimd.dma_start(out=flat_scr[:, :], in_=rows_sb[:, :]).then_inc(fsem, 16)
        c.bump(tag + "_fd", 16)
        gpsimd.wait_ge(fsem, c.at(tag + "_fd"))
        a_src = flat_scr.rearrange("p (h i) -> h p i", h=2)
        r2v = rows2.rearrange("a (h p i) -> a h p i", h=2, p=16)
        for r in range(2):
            gpsimd.dma_start(out=r2v[:, r, :, :],
                             in_=a_src[r:r + 1, :, :]).then_inc(fsem, 16)
            c.bump(tag + "_fd", 16)
    block.gpsimd(g1)
    gd_t = c.at(tag + "_fd")

    for r in range(2):
        for seg in range(NSEG):
            def tmm(tensor):
                if r == 0 and seg == 0:
                    tensor.wait_ge(fsem, gd_t)
                else:
                    tensor.wait_ge(dv, c.at("dv"))
                tensor.matmul(pbb[:, :], sm["onesrow"][:, :],
                              rows2[0:1, r * T + seg * 512:
                                    r * T + (seg + 1) * 512],
                              start=True, stop=True).then_inc(pe, 1)
                c.bump("pe")
            block.tensor(tmm)
            pe_b = c.at("pe")

            def vc(vector):
                vector.wait_ge(pe, pe_b)
                dst = aB if r == 0 else bB
                vector.tensor_copy(dst[:, seg * 512:(seg + 1) * 512],
                                   pbb[:, :]).then_inc(dv, 1)
                c.bump("dv")
            block.vector(vc)
    rowsB_t = c.at("dv")

    sbuf_in = c.sb(tag + "_nin", [128, 2, T], BF16)
    ntmp = c.sb(tag + "_ntmp", [128, T], BF16)
    nout = c.sb(tag + "_nout", [128, T], F32) if dest_mode == "out" else None
    scrr = s_scr.rearrange("(kc p) t -> p kc t", p=128)
    outr = (out_dram.rearrange("(kc p) t -> p kc t", p=128)
            if out_dram is not None else None)
    ld_cnt, nm_cnt, st_cnt = {}, {}, {}
    hd0 = c.at("hd")

    for oc in range(KC):
        def s1(sync):
            if oc == 0:
                sync.wait_ge(pe, last_stat)
                for t_ in store_tags:
                    wait_tag(sync, c, t_)
            if oc >= 2:
                sync.wait_ge(dv, nm_cnt[oc - 2])
            sync.dma_start(out=sbuf_in[:, oc % 2, :],
                           in_=scrr[:, oc, :]).then_inc(lsem[oc % 2], 16)
            ld_cnt[oc] = c.bump(lnames[oc % 2], 16)
        block.sync(s1)

        def v4(vector):
            vector.wait_ge(lsem[oc % 2], ld_cnt[oc])
            vector.wait_ge(dv, c.at("dv"))
            if oc == 0:
                vector.wait_ge(dv, rowsB_t)
            if dest_mode == "out" and oc >= 1:
                vector.wait_ge(osem, st_cnt[oc - 1])
            vector.tensor_tensor(out=ntmp[:], in0=sbuf_in[:, oc % 2, :],
                                 in1=aB[:], op=ALU.mult).then_inc(dv, 1)
            c.bump("dv")
            vector.wait_ge(dv, c.at("dv"))
            dst = nout[:] if dest_mode == "out" else h[:, oc, :]
            vector.tensor_tensor(out=dst, in0=ntmp[:], in1=bB[:],
                                 op=ALU.add).then_inc(dv, 1)
            nm_cnt[oc] = c.bump("dv")
        block.vector(v4)

        if dest_mode == "out":
            def s2(sync):
                sync.wait_ge(dv, nm_cnt[oc])
                sync.dma_start(out=outr[:, oc, :], in_=nout[:]).then_inc(osem, 16)
                st_cnt[oc] = c.bump(tag + "_od", 16)
            block.sync(s2)
        elif dest_mode == "both":
            def s2(sync):
                sync.wait_ge(dv, nm_cnt[oc])
                sync.dma_start(out=outr[:, oc, :],
                               in_=h[:, oc, :]).then_inc(osem, 16)
                st_cnt[oc] = c.bump(tag + "_od", 16)
            block.sync(s2)
    if dest_mode == "h":
        return ("dv", c.at("dv"))
    return (tag + "_od", c.at(tag + "_od"))



class QKSink(BaseSink):

    def __init__(self, c, tag, bias_colT, scr):
        self.tag = tag
        self.bias_colT = bias_colT
        self.scr = scr
        self.chunk = c.sb(tag + "_ck", [128, 2, T], BF16)
        self.ssem = [c.sem(tag + "_s0d"), c.sem(tag + "_s1d")]
        self.snames = [tag + "_s0d", tag + "_s1d"]
        self.evac_cnt = {}
        self.store_cnt = {}

    def store_tags(self, c):
        return [(n, c.at(n)) for n in self.snames]

    def evac(self, c, oc, tseg, psum):
        block = c.block
        pe, dv = c.sem("pe"), c.sem("dv")
        pe_t = c.at("pe")
        sb = self.chunk[:, oc % 2, tseg * 512:(tseg + 1) * 512]

        def v(vector):
            vector.wait_ge(pe, pe_t)
            if tseg == 0 and (oc - 2) in self.store_cnt:
                vector.wait_ge(self.ssem[oc % 2], self.store_cnt[oc - 2])
            vector.tensor_scalar(out=sb, in0=psum[:, :],
                                 scalar1=self.bias_colT[:, oc:oc + 1],
                                 scalar2=None, op0=ALU.add).then_inc(dv, 1)
            self.evac_cnt[(oc, tseg)] = c.bump("dv")
        block.vector(v)
        return ("dv", self.evac_cnt[(oc, tseg)])

    def oc_done(self, c, oc):
        block = c.block
        dv = c.sem("dv")
        dv_t = self.evac_cnt[(oc, NSEG - 1)]

        def s(sync):
            sync.wait_ge(dv, dv_t)
            sync.dma_start(out=self.scr[oc * 128:(oc + 1) * 128, :],
                           in_=self.chunk[:, oc % 2, :]).then_inc(
                               self.ssem[oc % 2], 16)
            self.store_cnt[oc] = c.bump(self.snames[oc % 2], 16)
        block.sync(s)


class VSink(BaseSink):

    def __init__(self, c, sm, tag, bias_colT, scr):
        self.sm = sm
        self.tag = tag
        self.bias_colT = bias_colT
        self.scr = scr
        self.vtmp = c.sb(tag + "_vt", [128, 2, 512], BF16)
        self.ttmp = c.sb(tag + "_tt", [128, 2, 512], BF16)
        self.pvt = c.ps(tag + "_pv", [128, 512], BF16)
        self.ssem = [c.sem(tag + "_s0d"), c.sem(tag + "_s1d")]
        self.snames = [tag + "_s0d", tag + "_s1d"]
        self.evac_cnt = {}
        self.tr_cnt = {}
        self.ev2_cnt = {}
        self.store_cnt = {}

    def store_tags(self, c):
        return [(n, c.at(n)) for n in self.snames]

    def evac(self, c, oc, tseg, psum):
        block = c.block
        pe, dv, ac = (c.sem(x) for x in ("pe", "dv", "ac"))
        pe_t = c.at("pe")
        u = (oc * NSEG + tseg) % 2
        sb = self.vtmp[:, u, :]

        def v(vector):
            vector.wait_ge(pe, pe_t)
            key = (oc * NSEG + tseg) - 2
            if key >= 0:
                vector.wait_ge(pe, self.tr_cnt[key])
            vector.tensor_scalar(out=sb, in0=psum[:, :],
                                 scalar1=self.bias_colT[:, oc:oc + 1],
                                 scalar2=None, op0=ALU.add).then_inc(dv, 1)
            self.evac_cnt[(oc, tseg)] = c.bump("dv")
        block.vector(v)
        dv_t = self.evac_cnt[(oc, tseg)]

        def t(tensor):
            tensor.wait_ge(dv, dv_t)
            key = (oc * NSEG + tseg) - 1
            if key >= 0:
                tensor.wait_ge(ac, self.ev2_cnt[key])
            last = None
            for j in range(4):
                last = tensor.transpose(self.pvt[:, j * 128:(j + 1) * 128],
                                        sb[:, j * 128:(j + 1) * 128],
                                        self.sm["identb"][:])
            last.then_inc(pe, 1)
            self.tr_cnt[oc * NSEG + tseg] = c.bump("pe")
        block.tensor(t)
        pe_t2 = c.at("pe")

        def a(scalar):
            scalar.wait_ge(pe, pe_t2)
            key = (oc * NSEG + tseg) - 2
            if key >= 0:
                scalar.wait_ge(self.ssem[u], self.store_cnt[key])
            scalar.copy(self.ttmp[:, u, :], self.pvt[:, :]).then_inc(ac, 1)
            self.ev2_cnt[oc * NSEG + tseg] = c.bump("ac")
        block.scalar(a)
        ac_t = c.at("ac")

        def s(sync):
            sync.wait_ge(ac, ac_t)
            tv = self.ttmp[:, u, :].rearrange("p (j e) -> p j e", j=4)
            dst = self.scr.rearrange("(tb p) e -> p tb e", p=128)
            sync.dma_start(
                out=dst[:, tseg * 4:(tseg + 1) * 4,
                        oc * 128:(oc + 1) * 128],
                in_=tv[:, :, :]).then_inc(self.ssem[u], 16)
            self.store_cnt[oc * NSEG + tseg] = c.bump(self.snames[u], 16)
        block.sync(s)
        return ("dv", dv_t)



class Res:
    def __init__(self):
        self.tags = []

    def used(self, tag):
        self.tags.append(tag)

    def wait(self, eng, c):
        for t in self.tags:
            wait_tag(eng, c, t)
        self.tags = []



def emit_l0_attention(c, sm, scr_q, scr_k, scr_v, ctxT, ready_tags=()):
    block = c.block
    pe, dv, ac = (c.sem(x) for x in ("pe", "dv", "ac"))
    hsem = [c.sem("att_h0d"), c.sem("att_h1d")]
    hnames = ["att_h0d", "att_h1d"]

    qh = c.sb("att_qh", [64, 2, T], BF16)
    kh = c.sb("att_kh", [64, 2, T], BF16)
    vh = c.sb("att_vh", [128, 2, NT, 64], BF16)
    P = c.sb("att_P", [128, 2, T], BF16)
    PT = c.sb("att_PT", [128, 2, NT, 128], BF16)
    negmax = c.sb("att_nm", [128, 2], F32)
    rowsum = c.sb("att_rs", [128, 2], F32)
    rr = c.sb("att_rr", [128, 2], F32)
    pS = c.ps("att_pS", [128, T], F32)
    pPT = c.ps("att_pPT", [128, T], BF16)
    pC = c.ps("att_pC", [128, 512], F32)

    scr_vr = scr_v.rearrange("(tb p) e -> p tb e", p=128)
    hload = {}
    r_hbuf = [Res(), Res()]
    r_pS = Res()
    r_P = [Res(), Res()]
    r_pPT = Res()
    r_PT = [Res(), Res()]
    r_pC = [Res(), Res(), Res(), Res()]
    r_ctxT = Res()

    def emit_front(h, hb, qt):
        u = qt % 2
        qsl = slice(qt * 128, (qt + 1) * 128)

        def t_s(tensor):
            if qt == 0:
                tensor.wait_ge(hsem[hb], hload[h])
            r_pS.wait(tensor, c)
            last = None
            for kseg in range(NSEG):
                last = tensor.matmul(pS[:, kseg * 512:(kseg + 1) * 512],
                                     qh[:, hb, qsl],
                                     kh[:, hb, kseg * 512:(kseg + 1) * 512],
                                     start=True, stop=True)
            last.then_inc(pe, 1)
            c.bump("pe")
        block.tensor(t_s)
        pe_t = c.at("pe")

        def a_e(scalar):
            scalar.wait_ge(pe, pe_t)
            r_P[u].wait(scalar, c)
            scalar.activation(out=P[:, u, :], in_=pS[:, :], func=AF.Exp,
                              bias=sm["neg60"][:, 0:1], scale=1.0,
                              accum_out=rowsum[:, u:u + 1]).then_inc(ac, 1)
            c.bump("ac")
        block.scalar(a_e)
        r_pS.used(("ac", c.at("ac")))
        ac_t = c.at("ac")

        def v_r(vector):
            vector.wait_ge(ac, ac_t)
            vector.reciprocal(rr[:, u:u + 1], rowsum[:, u:u + 1]).then_inc(dv, 1)
            c.bump("dv")
            vector.wait_ge(dv, c.at("dv"))
            vector.tensor_scalar(out=P[:, u, :], in0=P[:, u, :],
                                 scalar1=rr[:, u:u + 1], scalar2=1.0 / SQRT512,
                                 op0=ALU.mult, op1=ALU.mult).then_inc(dv, 1)
            c.bump("dv")
        block.vector(v_r)
        return c.at("dv")

    def emit_back(h, hb, qt, dv_t2):
        u = qt % 2
        qsl = slice(qt * 128, (qt + 1) * 128)

        def t_pt(tensor):
            tensor.wait_ge(dv, dv_t2)
            r_pPT.wait(tensor, c)
            last = None
            for kt in range(NT):
                last = tensor.transpose(pPT[:, kt * 128:(kt + 1) * 128],
                                        P[:, u, kt * 128:(kt + 1) * 128],
                                        sm["identb"][:])
            last.then_inc(pe, 1)
            c.bump("pe")
        block.tensor(t_pt)
        r_P[u].used(("pe", c.at("pe")))
        pe_t2 = c.at("pe")

        def a_pt(vector):
            vector.wait_ge(pe, pe_t2)
            r_PT[u].wait(vector, c)
            vector.tensor_copy(PT[:, u, :, :], pPT[:, :]).then_inc(dv, 1)
            c.bump("dv")
        block.vector(a_pt)
        r_pPT.used(("dv", c.at("dv")))
        ac_t2 = c.at("dv")

        def t_c(tensor):
            tensor.wait_ge(dv, ac_t2)
            r_pC[qt % 4].wait(tensor, c)
            tp = (0, 64) if (h % 2) else None
            outap = pC[(h % 2) * 64:(h % 2) * 64 + 64,
                       (qt % 4) * 128:(qt % 4) * 128 + 128]
            last = None
            for kt in range(NT):
                last = tensor.matmul(outap, vh[:, hb, kt, :],
                                     PT[:, u, kt, :],
                                     start=(kt == 0), stop=(kt == NT - 1),
                                     skip_group_check=True,
                                     tile_position=tp)
            last.then_inc(pe, 1)
            c.bump("pe")
        block.tensor(t_c)
        tagc = ("pe", c.at("pe"))
        r_PT[u].used(tagc)
        if qt == NT - 1:
            r_hbuf[hb].used(tagc)
        pe_t3 = c.at("pe")

        def v_c(vector):
            vector.wait_ge(pe, pe_t3)
            vector.tensor_copy(
                ctxT[(h % 2) * 64:(h % 2) * 64 + 64, h // 2, qsl],
                pC[(h % 2) * 64:(h % 2) * 64 + 64,
                   (qt % 4) * 128:(qt % 4) * 128 + 128]).then_inc(dv, 1)
            c.bump("dv")
        block.vector(v_c)
        r_pC[qt % 4].used(("dv", c.at("dv")))

    pending = []
    for h in range(8):
        hb = h % 2

        def s(sync):
            if h == 0:
                for rt in ready_tags:
                    wait_tag(sync, c, rt)
            r_hbuf[hb].wait(sync, c)
            sync.dma_start(out=qh[:, hb, :],
                           in_=scr_q[h * 64:(h + 1) * 64, :]).then_inc(hsem[hb], 16)
            c.bump(hnames[hb], 16)
            sync.dma_start(out=kh[:, hb, :],
                           in_=scr_k[h * 64:(h + 1) * 64, :]).then_inc(hsem[hb], 16)
            c.bump(hnames[hb], 16)
            sync.dma_start(out=vh[:, hb, :, :],
                           in_=scr_vr[:, :, h * 64:(h + 1) * 64]).then_inc(hsem[hb], 16)
            hload[h] = c.bump(hnames[hb], 16)
        block.sync(s)

        for qt in range(NT):
            dv_t2 = emit_front(h, hb, qt)
            pending.append((h, hb, qt, dv_t2))
            if len(pending) >= 2:
                emit_back(*pending.pop(0))
    while pending:
        emit_back(*pending.pop(0))
    return ("dv", c.at("dv"))



def emit_G_setup(c, sm, coefs0_d, mask0_d, coefs1_d, mask1_d, B_d, GT,
                 ready_tags=()):
    block = c.block
    pe, dv, ac = c.sem("pe"), c.sem("dv"), c.sem("ac")

    cbuf = c.sb("G_c", [128, 2, 3, 40], F32)
    mbuf = c.sb("G_m", [128, 2, 3, 40], U8)
    mf = c.sb("G_mf", [128, 3, 40], F32)
    W01b = c.sb("G_w01", [128, KC, 40], BF16)
    Bf32 = [c.sb(f"G_Bf{i}", [128, 2048], F32) for i in range(2)]
    Bbuf = c.sb("G_B", [128, 2, D], BF16)
    psG = c.ps("G_ps", [40, D], F32)

    c0r = coefs0_d.rearrange("e (oc p) r -> p oc e r", p=128)
    m0r = mask0_d.rearrange("e (oc p) r -> p oc e r", p=128)
    c1r = coefs1_d.rearrange("e (oc p) r -> p oc e r", p=128)
    m1r = mask1_d.rearrange("e (oc p) r -> p oc e r", p=128)

    csem = [c.sem("G_c0d"), c.sem("G_c1d")]
    cnames = ["G_c0d", "G_c1d"]
    r_cbuf = [Res(), Res()]
    for oc in range(KC):
        b = oc % 2

        def g(sync):
            if oc == 0:
                for rt in ready_tags:
                    wait_tag(sync, c, rt)
            r_cbuf[b].wait(sync, c)
            sync.dma_start(out=cbuf[:, b, :, 0:32],
                           in_=c0r[:, oc, :, :]).then_inc(csem[b], 16)
            c.bump(cnames[b], 16)
            sync.dma_start(out=cbuf[:, b, :, 32:40],
                           in_=c1r[:, oc, :, :]).then_inc(csem[b], 16)
            c.bump(cnames[b], 16)
            sync.dma_start(out=mbuf[:, b, :, 0:32],
                           in_=m0r[:, oc, :, :]).then_inc(csem[b], 16)
            c.bump(cnames[b], 16)
            sync.dma_start(out=mbuf[:, b, :, 32:40],
                           in_=m1r[:, oc, :, :]).then_inc(csem[b], 16)
            c.bump(cnames[b], 16)
        block.sync(g)
        gd_t = c.at(cnames[b])

        def v(vector):
            vector.wait_ge(csem[b], gd_t)
            vector.wait_ge(dv, c.at("dv"))
            vector.tensor_copy(mf[:], mbuf[:, b, :, :]).then_inc(dv, 1)
            c.bump("dv")
            vector.wait_ge(dv, c.at("dv"))
            vector.tensor_tensor(out=mf[:], in0=mf[:], in1=cbuf[:, b, :, :],
                                 op=ALU.mult).then_inc(dv, 1)
            c.bump("dv")
            vector.wait_ge(dv, c.at("dv"))
            vector.tensor_tensor(out=mf[:, 0, :], in0=mf[:, 0, :],
                                 in1=mf[:, 1, :], op=ALU.add).then_inc(dv, 1)
            c.bump("dv")
            vector.wait_ge(dv, c.at("dv"))
            vector.tensor_tensor(out=W01b[:, oc, :], in0=mf[:, 0, :],
                                 in1=mf[:, 2, :], op=ALU.add).then_inc(dv, 1)
            c.bump("dv")
        block.vector(v)
        r_cbuf[b].used(("dv", c.at("dv")))
    w01_t = c.at("dv")

    Br = B_d.rearrange("(jc p) t -> p jc t", p=128)
    bsem = [c.sem("G_b0d"), c.sem("G_b1d")]
    bnames = ["G_b0d", "G_b1d"]
    conv_done = {}
    mm_t = {}
    for jc in range(KC):
        b = jc % 2
        for half in range(2):
            def g2(sync):
                key = (jc - 1, half)
                if key in conv_done:
                    sync.wait_ge(ac, conv_done[key])
                sync.dma_start(
                    out=Bf32[half][:],
                    in_=Br[:, jc, half * 2048:(half + 1) * 2048]).then_inc(
                        bsem[half], 16)
                c.bump(bnames[half], 16)
            block.sync(g2)
            b_t = c.at(bnames[half])

            def a2(scalar):
                scalar.wait_ge(bsem[half], b_t)
                if jc >= 2 and half == 0:
                    scalar.wait_ge(pe, mm_t[jc - 2])
                scalar.copy(Bbuf[:, b, half * 2048:(half + 1) * 2048],
                            Bf32[half][:]).then_inc(ac, 1)
                conv_done[(jc, half)] = c.bump("ac")
            block.scalar(a2)
        cv_t = conv_done[(jc, 1)]

        def t(tensor):
            tensor.wait_ge(ac, cv_t)
            if jc == 0:
                tensor.wait_ge(dv, w01_t)
            last = None
            for seg in range(8):
                last = tensor.matmul(psG[0:40, seg * 512:(seg + 1) * 512],
                                     W01b[:, jc, :],
                                     Bbuf[:, b, seg * 512:(seg + 1) * 512],
                                     start=(jc == 0), stop=(jc == KC - 1),
                                     skip_group_check=True)
            last.then_inc(pe, 1)
            mm_t[jc] = c.bump("pe")
        block.tensor(t)
    pe_t = c.at("pe")

    def v2(vector):
        vector.wait_ge(pe, pe_t)
        vector.tensor_scalar(out=GT[0:40, :], in0=psG[0:40, :],
                             scalar1=0.5, scalar2=None,
                             op0=ALU.mult).then_inc(dv, 1)
        c.bump("dv")
    block.vector(v2)
    return ("dv", c.at("dv"))


def emit_eT(c, sm, encT_sb, h, eT, ready_tags=()):
    block = c.block
    pe, dv = c.sem("pe"), c.sem("dv")
    psE = c.ps("eT_ps", [40, T], F32)

    def t(tensor):
        for rt in ready_tags:
            wait_tag(tensor, c, rt)
        last = None
        for dc in range(KC):
            for seg in range(NSEG):
                last = tensor.matmul(psE[0:40, seg * 512:(seg + 1) * 512],
                                     encT_sb[:, dc, :],
                                     h[:, dc, seg * 512:(seg + 1) * 512],
                                     start=(dc == 0), stop=(dc == KC - 1),
                                     skip_group_check=True)
        last.then_inc(pe, 1)
        c.bump("pe")
    block.tensor(t)
    pe_t = c.at("pe")

    def v(vector):
        vector.wait_ge(pe, pe_t)
        vector.tensor_copy(eT[0:40, :], psE[0:40, :]).then_inc(dv, 1)
        c.bump("dv")
    block.vector(v)
    return ("dv", c.at("dv"))


def emit_l1mix(c, sm, scr_q1, scr_k1, v1tok, h, s_scr, psstat,
               ready_tags=()):
    block = c.block
    pe, dv, ac = (c.sem(x) for x in ("pe", "dv", "ac"))
    isem = c.sem("mx_id")
    s3sem = [c.sem("mx_s0d"), c.sem("mx_s1d")]
    s3names = ["mx_s0d", "mx_s1d"]

    qTt = c.sb("mx_q", [128, 1, KC, 128], BF16)
    kTt = c.sb("mx_k", [128, 1, KC, 128], BF16)
    vgrp = c.sb("mx_v", [32, 2, 16, 128], BF16)
    E = c.sb("mx_E", [32, 2, 512], F32)
    esum = c.sb("mx_es", [32, 2, 16], F32)
    Pn = c.sb("mx_Pn", [32, 2, 512], BF16)
    sbPT = c.sb("mx_PT", [32, 16, 32], BF16)
    s3blk = c.sb("mx_s3", [128, 2, KC, 128], BF16)
    sq3 = c.sb("mx_sq", [128, 1, KC, 128], BF16)
    pM = [c.ps("mx_pM0", [32, 512], F32), c.ps("mx_pM1", [32, 512], F32)]
    pPT2 = c.ps("mx_pPT", [32, 512], BF16)
    pC2 = c.ps("mx_pC", [128, 512], F32)

    qr = scr_q1.rearrange("(hc p) t -> p hc t", p=128)
    kr = scr_k1.rearrange("(hc p) t -> p hc t", p=128)
    vr = v1tok.rearrange("(tb t) (hp d) -> hp tb t d", t=128, d=128)
    s3r = s_scr.rearrange("(hc p) t -> p hc t", p=128)

    r_in = [Res()]
    r_vg = [Res(), Res()]
    vsem = [c.sem("mx_v0d"), c.sem("mx_v1d")]
    vnames = ["mx_v0d", "mx_v1d"]
    r_pM = [Res(), Res()]
    r_E = [Res(), Res()]
    r_Pn = [Res(), Res()]
    mpend = []
    r_pPT2 = Res()
    r_sbT = Res()
    r_pC2 = Res()
    r_s3 = [Res(), Res()]
    prev_stat = [0]
    stat_first = [True]

    for tt in range(NT):
        b3 = tt % 2
        b = 0

        def s(sync):
            if tt == 0:
                for rt in ready_tags:
                    wait_tag(sync, c, rt)
            r_in[b].wait(sync, c)
            tsl = slice(tt * 128, (tt + 1) * 128)
            sync.dma_start(out=qTt[:, b, :, :], in_=qr[:, :, tsl]).then_inc(isem, 16)
            c.bump("mx_id", 16)
            sync.dma_start(out=kTt[:, b, :, :], in_=kr[:, :, tsl]).then_inc(isem, 16)
            c.bump("mx_id", 16)
        block.sync(s)
        hd_t = c.at("mx_id")

        def mix_front(grp, gb):
            def s_v(sync):
                r_vg[gb].wait(sync, c)
                sync.dma_start(
                    out=vgrp[:, gb, :, :],
                    in_=vr[:, tt, grp * 16:(grp + 1) * 16, :]).then_inc(
                        vsem[gb], 16)
                c.bump(vnames[gb], 16)
            block.sync(s_v)
            v_t = c.at(vnames[gb])

            def t_s(tensor):
                if grp == 0:
                    tensor.wait_ge(isem, hd_t)
                r_pM[gb].wait(tensor, c)
                last = None
                for j in range(16):
                    tok = grp * 16 + j
                    last = tensor.matmul(pM[gb][0:32, j * 32:(j + 1) * 32],
                                         qTt[:, b, :, tok],
                                         kTt[:, b, :, tok],
                                         start=True, stop=True)
                last.then_inc(pe, 1)
                c.bump("pe")
            block.tensor(t_s)
            pe_t = c.at("pe")

            def a_e(scalar):
                scalar.wait_ge(pe, pe_t)
                r_E[gb].wait(scalar, c)
                scalar.activation(out=E[:, gb, :], in_=pM[gb][0:32, :],
                                  func=AF.Exp,
                                  scale=RSQRT128).then_inc(ac, 1)
                c.bump("ac")
            block.scalar(a_e)
            r_pM[gb].used(("ac", c.at("ac")))
            ac_t = c.at("ac")

            def v_n(vector):
                vector.wait_ge(ac, ac_t)
                Ev = E.rearrange("p g (tok hp) -> p g tok hp", hp=32)[:, gb]
                vector.tensor_reduce(out=esum[:, gb, :], in_=Ev[:, :, :],
                                     op=ALU.add, axis=AX.X).then_inc(dv, 1)
                c.bump("dv")
                vector.wait_ge(dv, c.at("dv"))
                vector.reciprocal(esum[:, gb, :], esum[:, gb, :]).then_inc(dv, 1)
                c.bump("dv")
                vector.wait_ge(dv, c.at("dv"))
                r_Pn[gb].wait(vector, c)
                Pnv = Pn.rearrange("p g (tok hp) -> p g tok hp", hp=32)
                vector.tensor_tensor(out=Pnv[:, gb, :, :], in0=Ev[:, :, :],
                                     in1=bcast_inner(esum[:, gb, :], 32),
                                     op=ALU.mult).then_inc(dv, 1)
                c.bump("dv")
            block.vector(v_n)
            r_E[gb].used(("dv", c.at("dv")))
            return (v_t, c.at("dv"))

        def mix_back(grp, gb, tags):
            v_t, dv_t = tags

            def t_tr(tensor):
                tensor.wait_ge(dv, dv_t)
                r_pPT2.wait(tensor, c)
                last = None
                for j in range(16):
                    last = tensor.transpose(pPT2[0:32, j * 32:(j + 1) * 32],
                                            Pn[0:32, gb, j * 32:(j + 1) * 32],
                                            sm["identb"][0:32, 0:32])
                last.then_inc(pe, 1)
                c.bump("pe")
            block.tensor(t_tr)
            pe_t2 = c.at("pe")

            def a_tr(scalar):
                scalar.wait_ge(pe, pe_t2)
                r_sbT.wait(scalar, c)
                scalar.copy(sbPT[:, :, :], pPT2[0:32, :]).then_inc(ac, 1)
                c.bump("ac")
            block.scalar(a_tr)
            r_pPT2.used(("ac", c.at("ac")))
            r_Pn[gb].used(("ac", c.at("ac")))
            ac_t2 = c.at("ac")

            def t_c(tensor):
                tensor.wait_ge(ac, ac_t2)
                tensor.wait_ge(vsem[gb], v_t)
                r_pC2.wait(tensor, c)
                last = None
                for j in range(16):
                    last = tensor.matmul(
                        pC2[:, j * 32:(j + 1) * 32],
                        vgrp[0:32, gb, j, :],
                        sbPT[0:32, j, :],
                        start=True, stop=True)
                last.then_inc(pe, 1)
                c.bump("pe")
            block.tensor(t_c)
            tag_c = ("pe", c.at("pe"))
            r_sbT.used(tag_c)
            r_vg[gb].used(tag_c)
            r_in[b].used(tag_c)
            pe_t3 = c.at("pe")

            def v_s3(vector):
                vector.wait_ge(pe, pe_t3)
                if grp == 0:
                    r_s3[b3].wait(vector, c)
                pv = bass.AP(pC2[:, :].tensor, pC2[:, :].offset,
                             [[512, 128], [1, 32], [32, 16]])
                vector.tensor_tensor(
                    out=s3blk[:, b3, :, grp * 16:(grp + 1) * 16],
                    in0=pv,
                    in1=h[:, :, tt * 128 + grp * 16: tt * 128 + (grp + 1) * 16],
                    op=ALU.add).then_inc(dv, 1)
                c.bump("dv")
            block.vector(v_s3)
            r_pC2.used(("dv", c.at("dv")))

        for grp in range(8):
            gb = grp % 2
            tags = mix_front(grp, gb)
            mpend.append((grp, gb, tags))
            if len(mpend) >= 2:
                mix_back(*mpend.pop(0))
        while mpend:
            mix_back(*mpend.pop(0))
        dv_blk = c.at("dv")

        def a_sq(scalar):
            scalar.wait_ge(dv, dv_blk)
            if tt > 0:
                scalar.wait_ge(pe, prev_stat[0])
            scalar.activation(out=sq3[:, 0, :, :], in_=s3blk[:, b3, :, :],
                              func=AF.Square).then_inc(ac, 1)
            c.bump("ac")
        block.scalar(a_sq)
        ac_sq = c.at("ac")

        def t_st(tensor):
            tensor.wait_ge(ac, ac_sq)
            last = None
            for hc in range(KC):
                tensor.matmul(psstat[:, tt:tt + 1],
                              s3blk[:, b3, hc, :], sm["onescol"][:, :],
                              start=(stat_first[0]), stop=False,
                              skip_group_check=True)
                stat_first[0] = False
                last = tensor.matmul(psstat[:, 16 + tt:17 + tt],
                                     sq3[:, 0, hc, :], sm["onescol"][:, :],
                                     start=False, stop=False,
                                     skip_group_check=True)
            last.then_inc(pe, 1)
            c.bump("pe")
        block.tensor(t_st)
        tag_st = ("pe", c.at("pe"))
        prev_stat[0] = c.at("pe")

        def s_st(sync):
            sync.wait_ge(dv, dv_blk)
            sync.dma_start(out=s3r[:, :, tt * 128:(tt + 1) * 128],
                           in_=s3blk[:, b3, :, :]).then_inc(s3sem[b3], 16)
            c.bump(s3names[b3], 16)
        block.sync(s_st)
        r_s3[b3].used(tag_st)
        r_s3[b3].used((s3names[b3], c.at(s3names[b3])))
    return [(n, c.at(n)) for n in s3names] + [("pe", c.at("pe"))]



def emit_dots_masks(c, sm, h, centT_sb, asg_d, cc_in_d, cc_out_d, flat_scr,
                    qB, rcolT, ready_tags=()):
    block = c.block
    pe, dv, ac, cc = (c.sem(x) for x in ("pe", "dv", "ac", "cc"))
    gd = c.sem("dt_gd")

    asg_sb = c.sb("dt_asg", [1, D], I32)
    asgf = c.sb("dt_asgf", [1, D], F32)
    cidx = c.sb("dt_cidx", [64, 1], F32)
    asgB = c.sb("dt_asgB", [64, 512], F32)
    ohT = c.sb("dt_ohT", [64, D], BF16)
    poh = c.ps("dt_poh", [64, 512], F32)

    def g0(gpsimd):
        for rt in ready_tags:
            wait_tag(gpsimd, c, rt)
        gpsimd.iota(cidx[:], base=0, pattern=[[0, 1]], channel_multiplier=1,
                    allow_small_or_imprecise_dtypes=True)
        gpsimd.dma_start(out=asg_sb[:], in_=asg_d[:, :]).then_inc(gd, 16)
        c.bump("dt_gd", 16)
    block.gpsimd(g0)
    gd_t = c.at("dt_gd")

    def v0(vector):
        vector.wait_ge(gd, gd_t)
        for rt in ready_tags:
            wait_tag(vector, c, rt)
        vector.tensor_copy(asgf[:], asg_sb[:]).then_inc(dv, 1)
        c.bump("dv")
    block.vector(v0)
    dv_t0 = c.at("dv")

    for seg in range(8):
        def t0(tensor):
            if seg == 0:
                tensor.wait_ge(dv, dv_t0)
            else:
                tensor.wait_ge(dv, c.at("dv"))
            tensor.matmul(poh[0:64, :], sm["onesrow"][0:1, 0:64],
                          asgf[0:1, seg * 512:(seg + 1) * 512],
                          start=True, stop=True).then_inc(pe, 1)
            c.bump("pe")
        block.tensor(t0)
        pe_t0 = c.at("pe")

        def v1(vector):
            vector.wait_ge(pe, pe_t0)
            vector.tensor_copy(asgB[:, :], poh[0:64, :]).then_inc(dv, 1)
            c.bump("dv")
            vector.wait_ge(dv, c.at("dv"))
            vector.tensor_scalar(out=ohT[0:64, seg * 512:(seg + 1) * 512],
                                 in0=asgB[:, :], scalar1=cidx[:, 0:1],
                                 scalar2=None, op0=ALU.is_equal).then_inc(dv, 1)
            c.bump("dv")
        block.vector(v1)
    oh_t = c.at("dv")

    ED = c.sb("dt_E", [128, 64], F32)
    dsum = c.sb("dt_ds", [128, 1], F32)
    dots = c.sb("dt_d", [128, 64], F32)
    dotb = c.sb("dt_db", [128, 64], BF16)
    dmax = c.sb("dt_dm", [128, 1], F32)
    qmAll = c.sb("dt_qm", [128, 16], F32)
    cmaxA = c.sb("dt_cm", [64, 128], F32)
    cml = c.sb("dt_cml", [64, 1], F32)
    cg = c.sb("dt_cg", [64, 1], F32)
    chitb = c.sb("dt_ch", [64, 1], BF16)
    pD = c.ps("dt_pD", [128, 64], F32)
    pDT = c.ps("dt_pDT", [64, 128], BF16)

    r_pD = Res()
    r_pDT = Res()

    for tt in range(NT):
        def t1(tensor):
            r_pD.wait(tensor, c)
            last = None
            for dc in range(KC):
                last = tensor.matmul(pD[:, :],
                                     h[:, dc, tt * 128:(tt + 1) * 128],
                                     centT_sb[:, dc, :],
                                     start=(dc == 0), stop=(dc == KC - 1),
                                     skip_group_check=True)
            last.then_inc(pe, 1)
            c.bump("pe")
        block.tensor(t1)
        pe_t1 = c.at("pe")

        def a1(scalar):
            scalar.wait_ge(pe, pe_t1)
            scalar.activation(out=ED[:, :], in_=pD[:, :], func=AF.Exp,
                              scale=10.0,
                              accum_out=dsum[:, 0:1]).then_inc(ac, 1)
            c.bump("ac")
        block.scalar(a1)
        r_pD.used(("ac", c.at("ac")))
        ac_t1 = c.at("ac")

        def v2(vector):
            vector.wait_ge(ac, ac_t1)
            vector.reciprocal(dsum[:], dsum[:]).then_inc(dv, 1)
            c.bump("dv")
            vector.wait_ge(dv, c.at("dv"))
            vector.tensor_scalar(out=dots[:, :], in0=ED[:, :],
                                 scalar1=dsum[:, 0:1], scalar2=None,
                                 op0=ALU.mult).then_inc(dv, 1)
            c.bump("dv")
            vector.wait_ge(dv, c.at("dv"))
            vector.tensor_reduce(out=dmax[:], in_=dots[:, :], op=ALU.max,
                                 axis=AX.X).then_inc(dv, 1)
            c.bump("dv")
            vector.wait_ge(dv, c.at("dv"))
            vector.tensor_scalar(out=qmAll[:, tt:tt + 1], in0=dmax[:],
                                 scalar1=0.01, scalar2=None, op0=ALU.is_gt)
            r_pDT.wait(vector, c)
            vector.tensor_copy(dotb[:, :], dots[:, :]).then_inc(dv, 1)
            c.bump("dv")
        block.vector(v2)
        dv_t2 = c.at("dv")

        def t2(tensor):
            tensor.wait_ge(dv, dv_t2)
            tensor.transpose(pDT[0:64, :], dotb[:, :],
                             sm["identb"][:]).then_inc(pe, 1)
            c.bump("pe")
        block.tensor(t2)
        pe_t2 = c.at("pe")

        def v3(vector):
            vector.wait_ge(pe, pe_t2)
            vector.wait_ge(dv, c.at("dv"))
            if tt == 0:
                vector.tensor_copy(cmaxA[:, :], pDT[0:64, :]).then_inc(dv, 1)
            else:
                vector.tensor_tensor(out=cmaxA[:, :], in0=cmaxA[:, :],
                                     in1=pDT[0:64, :],
                                     op=ALU.max).then_inc(dv, 1)
            c.bump("dv")
        block.vector(v3)
        r_pDT.used(("dv", c.at("dv")))
    dv_cm = c.at("dv")

    def v4(vector):
        vector.wait_ge(dv, dv_cm)
        vector.tensor_reduce(out=cml[:], in_=cmaxA[:, :], op=ALU.max,
                             axis=AX.X).then_inc(dv, 1)
        c.bump("dv")
    block.vector(v4)
    dv_t4 = c.at("dv")

    def g1(gpsimd):
        gpsimd.wait_ge(dv, dv_t4)
        gpsimd.dma_start(out=cc_in_d[:, :], in_=cml[:]).then_inc(gd, 16)
        c.bump("dt_gd", 16)
        gpsimd.wait_ge(gd, c.at("dt_gd"))
        gpsimd.collective_compute(
            "AllReduce", ALU.max,
            replica_groups=[[0, 1, 2, 3, 4, 5, 6, 7]],
            ins=[cc_in_d[:, :]],
            outs=[cc_out_d[:, :]],
        ).then_inc(cc, 1)
        c.bump("cc")
        gpsimd.wait_ge(cc, c.at("cc"))
        gpsimd.dma_start(out=cg[:], in_=cc_out_d[:, :]).then_inc(gd, 16)
        c.bump("dt_gd", 16)
    block.gpsimd(g1)
    gd_t1 = c.at("dt_gd")

    def v5(vector):
        vector.wait_ge(gd, gd_t1)
        vector.tensor_scalar(out=chitb[:], in0=cg[:], scalar1=0.01,
                             scalar2=None, op0=ALU.is_gt).then_inc(dv, 1)
        c.bump("dv")
    block.vector(v5)
    dv_t5 = c.at("dv")

    prm = c.ps("dt_prm", [128, 32], F32)

    def t3(tensor):
        tensor.wait_ge(dv, dv_t5)
        tensor.wait_ge(dv, oh_t)
        last = None
        for oc in range(KC):
            last = tensor.matmul(prm[:, oc:oc + 1],
                                 ohT[0:64, oc * 128:(oc + 1) * 128],
                                 chitb[0:64, 0:1],
                                 start=True, stop=True)
        last.then_inc(pe, 1)
        c.bump("pe")
    block.tensor(t3)
    pe_t3 = c.at("pe")

    def v6(vector):
        vector.wait_ge(pe, pe_t3)
        vector.tensor_copy(rcolT[:, :], prm[:, :]).then_inc(dv, 1)
        c.bump("dv")
    block.vector(v6)

    pqt = c.ps("dt_pqt", [16, 128], F32)
    qrow_sb = c.sb("dt_qrs", [16, 128], F32)
    qrow = c.sb("dt_qrow", [1, T], F32)
    pqb = c.ps("dt_pqb", [128, 512], F32)
    dv_qm = c.at("dv")

    def t4(tensor):
        tensor.wait_ge(dv, dv_qm)
        tensor.transpose(pqt[0:16, :], qmAll[:, :],
                         sm["identf"][:]).then_inc(pe, 1)
        c.bump("pe")
    block.tensor(t4)
    pe_t4 = c.at("pe")

    def v7(vector):
        vector.wait_ge(pe, pe_t4)
        vector.tensor_copy(qrow_sb[:, :], pqt[0:16, :]).then_inc(dv, 1)
        c.bump("dv")
    block.vector(v7)
    dv_t7 = c.at("dv")

    def g2(gpsimd):
        gpsimd.wait_ge(dv, dv_t7)
        gpsimd.dma_start(out=flat_scr[0:16, 0:128],
                         in_=qrow_sb[:, :]).then_inc(gd, 16)
        c.bump("dt_gd", 16)
        gpsimd.wait_ge(gd, c.at("dt_gd"))
        fsv = flat_scr.rearrange("p (a f) -> a p f", a=2)[0:1, :, :]
        qrv = qrow.rearrange("a (p f) -> a p f", p=16)
        gpsimd.dma_start(out=qrv[:, :, :], in_=fsv).then_inc(gd, 16)
        c.bump("dt_gd", 16)
    block.gpsimd(g2)
    gd_t2 = c.at("dt_gd")

    for seg in range(NSEG):
        def t5(tensor):
            if seg == 0:
                tensor.wait_ge(gd, gd_t2)
            else:
                tensor.wait_ge(dv, c.at("dv"))
            tensor.matmul(pqb[:, :], sm["onesrow"][:, :],
                          qrow[0:1, seg * 512:(seg + 1) * 512],
                          start=True, stop=True).then_inc(pe, 1)
            c.bump("pe")
        block.tensor(t5)
        pe_t5 = c.at("pe")

        def v8(vector):
            vector.wait_ge(pe, pe_t5)
            vector.tensor_copy(qB[:, seg * 512:(seg + 1) * 512],
                               pqb[:, :]).then_inc(dv, 1)
            c.bump("dv")
        block.vector(v8)
    return ("dv", c.at("dv"))


def emit_load_chunkedT(c, dst, src_dram, name, ready_tags=()):
    block = c.block
    gd = c.sem(name)
    srcr = src_dram.rearrange("(kc p) n -> p kc n", p=128)

    def g(gpsimd):
        for rt in ready_tags:
            wait_tag(gpsimd, c, rt)
        gpsimd.dma_start(out=dst[:], in_=srcr[:, :, :]).then_inc(gd, 16)
        c.bump(name, 16)
    block.gpsimd(g)
    return (name, c.at(name))


def finish(c):
    block = c.block

    def s(sync):
        for name in list(c.n):
            sync.wait_ge(c.sem(name), c.n[name])
    block.sync(s)



def build_l0attn():
    nc = bass.Bass("TRN2", target_bir_lowering=False)
    xT = nc.dram_tensor("xT", [D, T], F32, kind="ExternalInput").ap()
    WqT = nc.dram_tensor("WqT", [D, 512], F32, kind="ExternalInput").ap()
    WkT = nc.dram_tensor("WkT", [D, 512], F32, kind="ExternalInput").ap()
    WvT = nc.dram_tensor("WvT", [D, 512], F32, kind="ExternalInput").ap()
    bq = nc.dram_tensor("bq", [1, 512], F32, kind="ExternalInput").ap()
    bk = nc.dram_tensor("bk", [1, 512], F32, kind="ExternalInput").ap()
    bv = nc.dram_tensor("bv", [1, 512], F32, kind="ExternalInput").ap()
    WoT = nc.dram_tensor("WoT", [512, D], F32, kind="ExternalInput").ap()
    bo = nc.dram_tensor("bo", [1, D], F32, kind="ExternalInput").ap()
    h1T = nc.dram_tensor("h1T", [D, T], BF16, kind="ExternalOutput").ap()

    scr_q = nc.dram_tensor("scr_q", [512, T], BF16, kind="Internal").ap()
    scr_k = nc.dram_tensor("scr_k", [512, T], BF16, kind="Internal").ap()
    scr_v = nc.dram_tensor("scr_v", [T, 512], BF16, kind="Internal").ap()
    s_scr = nc.dram_tensor("s_scr", [D, T], BF16, kind="Internal").ap()
    flat_scr = nc.dram_tensor("flat_scr", [16, 256], F32, kind="Internal").ap()

    with ExitStack() as S:
        block = S.enter_context(nc.Block())
        c = Ctx(nc, S, block)
        sm = emit_smalls(c)
        h = c.sb("h", [128, KC, T], BF16)
        htag = emit_load_stream(c, h, xT, cast=True)

        bqT, t1 = emit_bias_T(c, sm, bq, 4, "bq")
        bkT, t2 = emit_bias_T(c, sm, bk, 4, "bk")
        bvT, t3 = emit_bias_T(c, sm, bv, 4, "bv")
        boT, t4 = emit_bias_T(c, sm, bo, 32, "bo")

        sink_tags = {}
        prev_stores = ()
        for (w_d, bT, tg, mksink) in (
                (WqT, bqT, "dq", lambda cc: QKSink(cc, "qs", bqT, scr_q)),
                (WkT, bkT, "dk", lambda cc: QKSink(cc, "ks", bkT, scr_k)),
                (WvT, bvT, "dv_", lambda cc: VSink(cc, sm, "vs", bvT, scr_v))):
            with ExitStack() as S1:
                c.S = S1
                prev = (("pe", c.at("pe")), ("dv", c.at("dv")),
                        ("ac", c.at("ac"))) + prev_stores
                sink = mksink(c)
                emit_dense(c, WT_dram=w_d, kchunks=KC, ochunks=4, moving=h,
                           sink=sink, tag=tg,
                           ready_tags=(htag, t1, t2, t3, t4) + prev)
                sink_tags[tg] = sink.store_tags(c)
                prev_stores = tuple(sink_tags[tg])
                c.S = S
        qsink_tags = sink_tags["dq"]
        ksink_tags = sink_tags["dk"]
        vsink_tags = sink_tags["dv_"]

        a1_tags = tuple(qsink_tags + ksink_tags + vsink_tags) + (
            ("pe", c.at("pe")), ("ac", c.at("ac")), ("dv", c.at("dv")))
        ctxT = c.sb("ctxT", [128, 4, T], BF16)
        with ExitStack() as S2:
            c.S = S2
            atag = emit_l0_attention(c, sm, scr_q, scr_k, scr_v, ctxT,
                                     ready_tags=a1_tags)
            c.S = S

        psstat0 = c.ps("ln0_st", [128, 32], F32)
        with ExitStack() as S3:
            c.S = S3
            lnsink = LNSink(c, sm, "ln0", h, boT, s_scr, psstat0)
            emit_dense(c, WT_dram=WoT, kchunks=4, ochunks=KC, moving=ctxT,
                       sink=lnsink, tag="do", ready_tags=(atag, t4))
            ln_tags = lnsink.store_tags(c)
            c.S = S
        with ExitStack() as S4:
            c.S = S4
            emit_normalize(c, sm, "n0", s_scr, h, psstat0, flat_scr,
                           dest_mode="both", out_dram=h1T,
                           store_tags=ln_tags)
            c.S = S
        finish(c)
    return nc


def build_l0ffn():
    nc = bass.Bass("TRN2", target_bir_lowering=False)
    h1T_d = nc.dram_tensor("h1T", [D, T], BF16, kind="ExternalInput").ap()
    WbT = nc.dram_tensor("WbT", [D, D], F32, kind="ExternalInput").ap()
    bb = nc.dram_tensor("bb", [1, D], F32, kind="ExternalInput").ap()
    encT = nc.dram_tensor("encT", [D, 40], F32, kind="ExternalInput").ap()
    coefs0 = nc.dram_tensor("coefs0", [3, D, 32], F32, kind="ExternalInput").ap()
    coefs1 = nc.dram_tensor("coefs1", [3, D, 8], F32, kind="ExternalInput").ap()
    mask0 = nc.dram_tensor("mask0", [3, D, 32], U8, kind="ExternalInput").ap()
    mask1 = nc.dram_tensor("mask1", [3, D, 8], U8, kind="ExternalInput").ap()
    Bc = nc.dram_tensor("Bc", [D, D], F32, kind="ExternalInput").ap()
    h2T = nc.dram_tensor("h2T", [D, T], BF16, kind="ExternalOutput").ap()

    s_scr = nc.dram_tensor("s_scr", [D, T], BF16, kind="Internal").ap()
    flat_scr = nc.dram_tensor("flat_scr", [16, 256], F32, kind="Internal").ap()

    with ExitStack() as S:
        block = S.enter_context(nc.Block())
        c = Ctx(nc, S, block)
        sm = emit_smalls(c)
        GT = c.sb("GT", [40, D], BF16)
        eT = c.sb("eT", [40, T], BF16)
        encT_sb = c.sb("encT_sb", [128, KC, 40], BF16)
        h = c.sb("h", [128, KC, T], BF16)

        etag = emit_load_chunkedT(c, encT_sb, encT, "enc_d")
        htag = emit_load_stream(c, h, h1T_d, cast=False)
        bbT, t1 = emit_bias_T(c, sm, bb, 32, "bb")

        with ExitStack() as S1:
            c.S = S1
            gtag = emit_G_setup(c, sm, coefs0, mask0, coefs1, mask1, Bc, GT,
                                ready_tags=(sm["ready"], t1))
            c.S = S

        psstat1 = c.ps("ln1_st", [128, 32], F32)
        with ExitStack() as S2:
            c.S = S2
            with ExitStack() as S2e:
                c.S = S2e
                ettag = emit_eT(c, sm, encT_sb, h, eT,
                                ready_tags=(etag, htag, gtag))
                c.S = S2

            def upd(tensor, oc, tseg, psum):
                return tensor.matmul(
                    psum[:, :], GT[0:40, oc * 128:(oc + 1) * 128],
                    eT[0:40, tseg * 512:(tseg + 1) * 512],
                    start=False, stop=True, skip_group_check=True)

            lnsink = LNSink(c, sm, "ln1", h, bbT, s_scr, psstat1)
            emit_dense(c, WT_dram=WbT, kchunks=KC, ochunks=KC, moving=h,
                       sink=lnsink, extra_mm=upd, tag="db",
                       ready_tags=(gtag, ettag, t1))
            ln_tags = lnsink.store_tags(c)
            c.S = S
        with ExitStack() as S2n:
            c.S = S2n
            emit_normalize(c, sm, "n1", s_scr, h, psstat1, flat_scr,
                           dest_mode="both", out_dram=h2T,
                           store_tags=ln_tags)
            c.S = S
        finish(c)
    return nc


def build_l1qkv():
    nc = bass.Bass("TRN2", target_bir_lowering=False)
    h2T_d = nc.dram_tensor("h2T", [D, T], BF16, kind="ExternalInput").ap()
    wqT = nc.dram_tensor("wqT", [D, D], F32, kind="ExternalInput").ap()
    wkT = nc.dram_tensor("wkT", [D, D], F32, kind="ExternalInput").ap()
    wvT = nc.dram_tensor("wvT", [D, D], F32, kind="ExternalInput").ap()
    bq = nc.dram_tensor("bq", [1, D], F32, kind="ExternalInput").ap()
    bk = nc.dram_tensor("bk", [1, D], F32, kind="ExternalInput").ap()
    bv = nc.dram_tensor("bv", [1, D], F32, kind="ExternalInput").ap()
    q1 = nc.dram_tensor("q1", [D, T], BF16, kind="ExternalOutput").ap()
    k1 = nc.dram_tensor("k1", [D, T], BF16, kind="ExternalOutput").ap()
    v1 = nc.dram_tensor("v1", [T, D], BF16, kind="ExternalOutput").ap()

    with ExitStack() as S:
        block = S.enter_context(nc.Block())
        c = Ctx(nc, S, block)
        sm = emit_smalls(c)
        h = c.sb("h", [128, KC, T], BF16)
        htag = emit_load_stream(c, h, h2T_d, cast=False)
        bqT, t1 = emit_bias_T(c, sm, bq, 32, "bq")
        bkT, t2 = emit_bias_T(c, sm, bk, 32, "bk")
        bvT, t3 = emit_bias_T(c, sm, bv, 32, "bv")

        prev_stores = ()
        for (w_d, bT, out_d, tg, rt) in ((wqT, bqT, q1, "pq", t1),
                                         (wkT, bkT, k1, "pk", t2),
                                         (wvT, bvT, v1, "pv", t3)):
            with ExitStack() as S1:
                c.S = S1
                prev = (("pe", c.at("pe")), ("dv", c.at("dv")),
                        ("ac", c.at("ac"))) + prev_stores
                if tg == "pv":
                    sink = VSink(c, sm, tg + "s", bT, out_d)
                else:
                    sink = QKSink(c, tg + "s", bT, out_d)
                emit_dense(c, WT_dram=w_d, kchunks=KC, ochunks=KC, moving=h,
                           sink=sink, tag=tg,
                           ready_tags=(htag, t1, t2, t3) + prev)
                prev_stores = tuple(sink.store_tags(c))
                c.S = S
        finish(c)
    return nc


def build_l1mix():
    nc = bass.Bass("TRN2", target_bir_lowering=False)
    h2T_d = nc.dram_tensor("h2T", [D, T], BF16, kind="ExternalInput").ap()
    q1 = nc.dram_tensor("q1", [D, T], BF16, kind="ExternalInput").ap()
    k1 = nc.dram_tensor("k1", [D, T], BF16, kind="ExternalInput").ap()
    v1 = nc.dram_tensor("v1", [T, D], BF16, kind="ExternalInput").ap()
    h3T = nc.dram_tensor("h3T", [D, T], BF16, kind="ExternalOutput").ap()
    s_scr = nc.dram_tensor("s_scr", [D, T], BF16, kind="Internal").ap()
    flat_scr = nc.dram_tensor("flat_scr", [16, 256], F32, kind="Internal").ap()

    with ExitStack() as S:
        block = S.enter_context(nc.Block())
        c = Ctx(nc, S, block)
        sm = emit_smalls(c)
        h = c.sb("h", [128, KC, T], BF16)
        htag = emit_load_stream(c, h, h2T_d, cast=False)
        psstat = c.ps("mx_st", [128, 32], F32)
        with ExitStack() as S1:
            c.S = S1
            mix_tags = emit_l1mix(c, sm, q1, k1, v1, h, s_scr, psstat,
                                  ready_tags=(htag, sm["ready"]))
            c.S = S
        with ExitStack() as S2:
            c.S = S2
            emit_normalize(c, sm, "n2", s_scr, h, psstat, flat_scr,
                           dest_mode="both", out_dram=h3T,
                           store_tags=mix_tags)
            c.S = S
        finish(c)
    return nc


def build_l1ffn():
    nc = bass.Bass("TRN2", target_bir_lowering=False)
    h3T_d = nc.dram_tensor("h3T", [D, T], BF16, kind="ExternalInput").ap()
    WdT = nc.dram_tensor("WdT", [D, D], F32, kind="ExternalInput").ap()
    bd = nc.dram_tensor("bd", [1, D], F32, kind="ExternalInput").ap()
    centT = nc.dram_tensor("centT", [D, 64], F32, kind="ExternalInput").ap()
    asg = nc.dram_tensor("asg", [1, D], I32, kind="ExternalInput").ap()
    outT = nc.dram_tensor("outT", [D, T], F32, kind="ExternalOutput").ap()

    s_scr = nc.dram_tensor("s_scr", [D, T], BF16, kind="Internal").ap()
    flat_scr = nc.dram_tensor("flat_scr", [16, 256], F32, kind="Internal").ap()
    cc_in = nc.dram_tensor("cc_in", [64, 1], F32, kind="Internal").ap()
    cc_out = nc.dram_tensor("cc_out", [64, 1], F32, kind="Internal").ap()

    with ExitStack() as S:
        block = S.enter_context(nc.Block())
        c = Ctx(nc, S, block)
        sm = emit_smalls(c)
        h = c.sb("h", [128, KC, T], BF16)
        htag = emit_load_stream(c, h, h3T_d, cast=False)
        centT_sb = c.sb("centT_sb", [128, KC, 64], BF16)
        ctag = emit_load_chunkedT(c, centT_sb, centT, "cen_d")
        bdT, t1 = emit_bias_T(c, sm, bd, 32, "bd")
        qB = c.sb("qB", [128, T], BF16)
        rcolT = c.sb("rcolT", [128, KC], F32)

        with ExitStack() as S1:
            c.S = S1
            mtag = emit_dots_masks(c, sm, h, centT_sb, asg, cc_in, cc_out,
                                   flat_scr, qB, rcolT,
                                   ready_tags=(htag, ctag, t1, sm["ready"]))
            c.S = S

        psstat3 = c.ps("ln3_st", [128, 32], F32)
        with ExitStack() as S2:
            c.S = S2
            lnsink = LNSink(c, sm, "ln3", h, bdT, s_scr, psstat3,
                            mask=(rcolT, qB))
            emit_dense(c, WT_dram=WdT, kchunks=KC, ochunks=KC, moving=h,
                       sink=lnsink, tag="dd", ready_tags=(mtag, t1))
            ln_tags = lnsink.store_tags(c)
            c.S = S
        with ExitStack() as S2n:
            c.S = S2n
            emit_normalize(c, sm, "n3", s_scr, h, psstat3, flat_scr,
                           dest_mode="out", out_dram=outT,
                           store_tags=ln_tags)
            c.S = S
        finish(c)
    return nc



def _np(x):
    return np.asarray(x)


def _bf(x):
    return np.ascontiguousarray(x)


def prep_host(inputs):
    P = {}
    x = _np(inputs["L0_attn_x"]).astype(np.float32, copy=False)
    P["xT"] = [np.ascontiguousarray(x[b].T) for b in range(8)]
    Wqk = _np(inputs["L0_attn_Wqk_w"])
    P["WqT"] = np.ascontiguousarray(Wqk[0::2].T)
    P["WkT"] = np.ascontiguousarray(Wqk[1::2].T)
    bqk = _np(inputs["L0_attn_Wqk_b"])
    P["bq0"] = np.ascontiguousarray(bqk[0::2].reshape(1, 512))
    P["bk0"] = np.ascontiguousarray(bqk[1::2].reshape(1, 512))
    P["WvT"] = np.ascontiguousarray(_np(inputs["L0_attn_Wv_w"]).T)
    P["bv0"] = _np(inputs["L0_attn_Wv_b"]).reshape(1, 512)
    P["WoT"] = np.ascontiguousarray(_np(inputs["L0_attn_Wo_w"]).T)
    P["bo0"] = _np(inputs["L0_attn_Wo_b"]).reshape(1, D)

    P["WbT"] = np.ascontiguousarray(_np(inputs["L0_ffn_W_base"]).T)
    P["bb"] = _np(inputs["L0_ffn_b_base"]).reshape(1, D)
    enc = np.concatenate([_np(inputs["L0_ffn_enc0"]),
                          _np(inputs["L0_ffn_enc1"])], axis=0)
    P["encT"] = np.ascontiguousarray(enc.T)
    P["coefs0"] = np.ascontiguousarray(_np(inputs["L0_ffn_coefs0"]))
    P["coefs1"] = np.ascontiguousarray(_np(inputs["L0_ffn_coefs1"]))
    P["mask0"] = np.ascontiguousarray(_np(inputs["L0_ffn_mask0"])).view(np.uint8)
    P["mask1"] = np.ascontiguousarray(_np(inputs["L0_ffn_mask1"])).view(np.uint8)
    P["Bc"] = fourier_basis_f32()

    P["wqT"] = np.ascontiguousarray(_np(inputs["L1_attn_wq"]).T)
    P["wkT"] = np.ascontiguousarray(_np(inputs["L1_attn_wk"]).T)
    P["wvT"] = np.ascontiguousarray(_np(inputs["L1_attn_wv"]).T)
    P["bq1"] = _np(inputs["L1_attn_bq"]).reshape(1, D)
    P["bk1"] = _np(inputs["L1_attn_bk"]).reshape(1, D)
    P["bv1"] = _np(inputs["L1_attn_bv"]).reshape(1, D)

    P["WdT"] = np.ascontiguousarray(_np(inputs["L1_ffn_weight"]).T)
    P["bd"] = _np(inputs["L1_ffn_bias"]).reshape(1, D)
    P["centT"] = np.ascontiguousarray(_np(inputs["L1_ffn_centroids"]).T)
    P["asg"] = _np(inputs["L1_ffn_assignments"]).astype(np.int32).reshape(1, D)
    return P


def kernel5(**inputs):
    P = prep_host(inputs)
    cores = list(range(8))

    nc1 = build_l0attn()
    r1 = run_bass_kernel_spmd(nc1, [dict(
        xT=P["xT"][b], WqT=P["WqT"], WkT=P["WkT"], WvT=P["WvT"],
        bq=P["bq0"], bk=P["bk0"], bv=P["bv0"], WoT=P["WoT"], bo=P["bo0"],
    ) for b in range(8)], core_ids=cores)
    h1 = [r1.results[b]["h1T"] for b in range(8)]

    nc2 = build_l0ffn()
    r2 = run_bass_kernel_spmd(nc2, [dict(
        h1T=h1[b], WbT=P["WbT"], bb=P["bb"], encT=P["encT"],
        coefs0=P["coefs0"], coefs1=P["coefs1"], mask0=P["mask0"],
        mask1=P["mask1"], Bc=P["Bc"],
    ) for b in range(8)], core_ids=cores)
    h2 = [r2.results[b]["h2T"] for b in range(8)]

    nc3 = build_l1qkv()
    r3 = run_bass_kernel_spmd(nc3, [dict(
        h2T=h2[b], wqT=P["wqT"], wkT=P["wkT"], wvT=P["wvT"],
        bq=P["bq1"], bk=P["bk1"], bv=P["bv1"],
    ) for b in range(8)], core_ids=cores)

    nc4 = build_l1mix()
    r4 = run_bass_kernel_spmd(nc4, [dict(
        h2T=h2[b], q1=r3.results[b]["q1"], k1=r3.results[b]["k1"],
        v1=r3.results[b]["v1"],
    ) for b in range(8)], core_ids=cores)
    h3 = [r4.results[b]["h3T"] for b in range(8)]

    nc5 = build_l1ffn()
    r5 = run_bass_kernel_spmd(nc5, [dict(
        h3T=h3[b], WdT=P["WdT"], bd=P["bd"], centT=P["centT"], asg=P["asg"],
    ) for b in range(8)], core_ids=cores)

    out = np.empty((8, T, D), np.float32)
    for b in range(8):
        out[b] = np.asarray(r5.results[b]["outT"]).T
    return out


def barrier(c, *extra):
    tags = [(n, c.n[n]) for n in c.n if c.n[n] > 0 and n != "cc"]
    for e in extra:
        tags.extend(e)
    return tuple(tags)


def build_fused(debug=False, skip=()):
    nc = bass.Bass("TRN2", target_bir_lowering=False)
    AP_ = lambda n, s, d, k: nc.dram_tensor(n, s, d, kind=k).ap()
    xT = AP_("xT", [D, T], F32, "ExternalInput")
    WqT = AP_("WqT", [D, 512], F32, "ExternalInput")
    WkT = AP_("WkT", [D, 512], F32, "ExternalInput")
    WvT = AP_("WvT", [D, 512], F32, "ExternalInput")
    bq0 = AP_("bq0", [1, 512], F32, "ExternalInput")
    bk0 = AP_("bk0", [1, 512], F32, "ExternalInput")
    bv0 = AP_("bv0", [1, 512], F32, "ExternalInput")
    WoT = AP_("WoT", [512, D], F32, "ExternalInput")
    bo0 = AP_("bo0", [1, D], F32, "ExternalInput")
    WbT = AP_("WbT", [D, D], F32, "ExternalInput")
    bb = AP_("bb", [1, D], F32, "ExternalInput")
    encT = AP_("encT", [D, 40], F32, "ExternalInput")
    coefs0 = AP_("coefs0", [3, D, 32], F32, "ExternalInput")
    coefs1 = AP_("coefs1", [3, D, 8], F32, "ExternalInput")
    mask0 = AP_("mask0", [3, D, 32], U8, "ExternalInput")
    mask1 = AP_("mask1", [3, D, 8], U8, "ExternalInput")
    Bc = AP_("Bc", [D, D], F32, "ExternalInput")
    wqT = AP_("wqT", [D, D], F32, "ExternalInput")
    wkT = AP_("wkT", [D, D], F32, "ExternalInput")
    wvT = AP_("wvT", [D, D], F32, "ExternalInput")
    bq1 = AP_("bq1", [1, D], F32, "ExternalInput")
    bk1 = AP_("bk1", [1, D], F32, "ExternalInput")
    bv1 = AP_("bv1", [1, D], F32, "ExternalInput")
    WdT = AP_("WdT", [D, D], F32, "ExternalInput")
    bd = AP_("bd", [1, D], F32, "ExternalInput")
    centT = AP_("centT", [D, 64], F32, "ExternalInput")
    asg = AP_("asg", [1, D], I32, "ExternalInput")
    outT = AP_("outT", [D, T], F32, "ExternalOutput")

    scr_q = AP_("scr_q", [512, T], BF16, "Internal")
    scr_k = AP_("scr_k", [512, T], BF16, "Internal")
    scr_v = AP_("scr_v", [T, 512], BF16, "Internal")
    scr_q1 = AP_("scr_q1", [D, T], BF16, "Internal")
    scr_k1 = AP_("scr_k1", [D, T], BF16, "Internal")
    scr_v1 = AP_("scr_v1", [T, D], BF16, "Internal")
    s_scr = AP_("s_scr", [D, T], BF16, "Internal")
    flat0 = AP_("flat0", [16, 256], F32, "Internal")
    flat1 = AP_("flat1", [16, 256], F32, "Internal")
    flat2 = AP_("flat2", [16, 256], F32, "Internal")
    flat3 = AP_("flat3", [16, 256], F32, "Internal")
    cc_in = AP_("cc_in", [64, 1], F32, "Internal")
    cc_out = AP_("cc_out", [64, 1], F32, "Internal")

    dbgs = {}
    if debug:
        for nm in ("dbg0", "dbg1", "dbg2"):
            dbgs[nm] = AP_(nm, [D, T], BF16, "ExternalOutput")

    with ExitStack() as S:
        block = S.enter_context(nc.Block())
        c = Ctx(nc, S, block)
        sm = emit_smalls(c)
        h = c.sb("h", [128, KC, T], BF16)
        GT = c.sb("GT", [40, D], BF16)

        def dump_h(nm, tag):
            if not debug:
                return
            dd = dbgs[nm].rearrange("(kc p) t -> p kc t", p=128)
            sem = c.sem(nm + "_d")

            def s_(sync):
                wait_tag(sync, c, tag)
                sync.dma_start(out=dd[:, :, :], in_=h[:]).then_inc(sem, 16)
                c.bump(nm + "_d", 16)
            block.sync(s_)

        with ExitStack() as S0:
            c.S = S0
            gtag = emit_G_setup(c, sm, coefs0, mask0, coefs1, mask1, Bc, GT,
                                ready_tags=(sm["ready"],))
            c.S = S
        htag = emit_load_stream(c, h, xT, cast=True)

        bqT, t1 = emit_bias_T(c, sm, bq0, 4, "bq")
        bkT, t2 = emit_bias_T(c, sm, bk0, 4, "bk")
        bvT, t3 = emit_bias_T(c, sm, bv0, 4, "bv")
        boT, t4 = emit_bias_T(c, sm, bo0, 32, "bo")
        bias_tags = (t1, t2, t3, t4, gtag)

        sink_tags = {}
        prev_stores = ()
        for (w_d, tg, mksink) in (
                (WqT, "dq", lambda cc: QKSink(cc, "qs", bqT, scr_q)),
                (WkT, "dk", lambda cc: QKSink(cc, "ks", bkT, scr_k)),
                (WvT, "dv_", lambda cc: VSink(cc, sm, "vs", bvT, scr_v))):
            with ExitStack() as S1:
                c.S = S1
                prev = barrier(c) + prev_stores
                sink = mksink(c)
                emit_dense(c, WT_dram=w_d, kchunks=KC, ochunks=4, moving=h,
                           sink=sink, tag=tg,
                           ready_tags=(htag,) + bias_tags + prev)
                sink_tags[tg] = sink.store_tags(c)
                prev_stores = tuple(sink_tags[tg])
                c.S = S
        a1_tags = barrier(c, sink_tags["dq"], sink_tags["dk"],
                          sink_tags["dv_"])

        SL0 = ExitStack()
        c.S = SL0
        psstat0 = c.ps("ln0_st", [128, 32], F32)
        with ExitStack() as SA:
            c.S = SA
            ctxT = c.sb("ctxT", [128, 4, T], BF16)
            with ExitStack() as S2:
                c.S = S2
                atag = emit_l0_attention(c, sm, scr_q, scr_k, scr_v, ctxT,
                                         ready_tags=a1_tags)
                c.S = SA
            with ExitStack() as S3:
                c.S = S3
                lnsink = LNSink(c, sm, "ln0", h, boT, s_scr, psstat0)
                emit_dense(c, WT_dram=WoT, kchunks=4, ochunks=KC, moving=ctxT,
                           sink=lnsink, tag="do",
                           ready_tags=(atag,) + barrier(c))
                ln_tags = lnsink.store_tags(c)
                c.S = SA
            c.S = SL0
        with ExitStack() as S4:
            c.S = S4
            n0tag = emit_normalize(c, sm, "n0", s_scr, h, psstat0, flat0,
                                   dest_mode="h", store_tags=ln_tags)
            dump_h("dbg0", n0tag)
            c.S = SL0
        SL0.close()
        c.S = S

        with ExitStack() as SF:
            c.S = SF
            encT_sb = c.sb("encT_sb", [128, KC, 40], BF16)
            eT = c.sb("eT", [40, T], BF16)
            bar0 = barrier(c)
            etag = emit_load_chunkedT(c, encT_sb, encT, "enc_d",
                                      ready_tags=bar0)
            bbT, tb1 = emit_bias_T(c, sm, bb, 32, "bb", ready_tags=bar0)
            psstat1 = c.ps("ln1_st", [128, 32], F32)
            with ExitStack() as SFe:
                c.S = SFe
                if "eT" in skip:
                    gpx = c.sem("gp")
                    def _z(gpsimd):
                        wait_tag(gpsimd, c, n0tag)
                        gpsimd.memset(eT[0:40, :], 0.0).then_inc(gpx, 1)
                        c.bump("gp")
                    c.block.gpsimd(_z)
                    ettag = ("gp", c.at("gp"))
                else:
                    ettag = emit_eT(c, sm, encT_sb, h, eT,
                                    ready_tags=(etag, n0tag) + barrier(c))
                c.S = SF

            def upd(tensor, oc, tseg, psum):
                return tensor.matmul(
                    psum[:, :], GT[0:40, oc * 128:(oc + 1) * 128],
                    eT[0:40, tseg * 512:(tseg + 1) * 512],
                    start=False, stop=True, skip_group_check=True)

            with ExitStack() as SFd:
                c.S = SFd
                lnsink1 = LNSink(c, sm, "ln1", h, bbT, s_scr, psstat1)
                emit_dense(c, WT_dram=WbT, kchunks=KC, ochunks=KC, moving=h,
                           sink=lnsink1,
                           extra_mm=(None if "upd" in skip else upd),
                           tag="db",
                           ready_tags=(ettag, tb1) + barrier(c))
                ln1_tags = lnsink1.store_tags(c)
                c.S = SF
            with ExitStack() as SFn:
                c.S = SFn
                n1tag = emit_normalize(c, sm, "n1", s_scr, h, psstat1, flat1,
                                       dest_mode="h", store_tags=ln1_tags)
                dump_h("dbg1", n1tag)
                c.S = SF
            c.S = S

        prev_stores = ()
        qkv_tags = {}
        with ExitStack() as SQ:
            c.S = SQ
            barq = barrier(c)
            bq1T, tq1 = emit_bias_T(c, sm, bq1, 32, "b1q", ready_tags=barq)
            bk1T, tq2 = emit_bias_T(c, sm, bk1, 32, "b1k")
            bv1T, tq3 = emit_bias_T(c, sm, bv1, 32, "b1v")
            for (w_d, bT, out_d, tg) in ((wqT, bq1T, scr_q1, "pq"),
                                         (wkT, bk1T, scr_k1, "pk"),
                                         (wvT, bv1T, scr_v1, "pv")):
                with ExitStack() as S1:
                    c.S = S1
                    prev = barrier(c) + prev_stores
                    if tg == "pv":
                        sink = VSink(c, sm, tg + "s", bT, out_d)
                    else:
                        sink = QKSink(c, tg + "s", bT, out_d)
                    emit_dense(c, WT_dram=w_d, kchunks=KC, ochunks=KC,
                               moving=h, sink=sink, tag=tg,
                               ready_tags=(n1tag, tq1, tq2, tq3) + prev)
                    qkv_tags[tg] = sink.store_tags(c)
                    prev_stores = tuple(qkv_tags[tg])
                    c.S = S
            c.S = S

        SL2 = ExitStack()
        c.S = SL2
        psstat2 = c.ps("ln2_st", [128, 32], F32)
        with ExitStack() as SM:
            c.S = SM
            mix_tags = emit_l1mix(
                c, sm, scr_q1, scr_k1, scr_v1, h, s_scr, psstat2,
                ready_tags=barrier(c, qkv_tags["pq"], qkv_tags["pk"],
                                   qkv_tags["pv"]))
            c.S = SL2
        with ExitStack() as SMn:
            c.S = SMn
            n2tag = emit_normalize(c, sm, "n2", s_scr, h, psstat2, flat2,
                                   dest_mode="h", store_tags=mix_tags)
            dump_h("dbg2", n2tag)
            c.S = SL2
        SL2.close()
        c.S = S

        with ExitStack() as SD:
            c.S = SD
            centT_sb = c.sb("centT_sb", [128, KC, 64], BF16)
            bard = barrier(c)
            ctag = emit_load_chunkedT(c, centT_sb, centT, "cen_d",
                                      ready_tags=bard)
            bdT, td1 = emit_bias_T(c, sm, bd, 32, "bd", ready_tags=bard)
            qB = c.sb("qB", [128, T], BF16)
            rcolT = c.sb("rcolT", [128, KC], F32)
            psstat3 = c.ps("ln3_st", [128, 32], F32)
            with ExitStack() as SD1:
                c.S = SD1
                mtag = emit_dots_masks(
                    c, sm, h, centT_sb, asg, cc_in, cc_out, flat3, qB, rcolT,
                    ready_tags=(n2tag, ctag, td1) + barrier(c))
                c.S = SD
            with ExitStack() as SD2:
                c.S = SD2
                lnsink3 = LNSink(c, sm, "ln3", h, bdT, s_scr, psstat3,
                                 mask=(rcolT, qB))
                emit_dense(c, WT_dram=WdT, kchunks=KC, ochunks=KC, moving=h,
                           sink=lnsink3, tag="dd",
                           ready_tags=(mtag,) + barrier(c))
                ln3_tags = lnsink3.store_tags(c)
                c.S = SD
            with ExitStack() as SD3:
                c.S = SD3
                emit_normalize(c, sm, "n3", s_scr, h, psstat3, flat3,
                               dest_mode="out", out_dram=outT,
                               store_tags=ln3_tags)
                c.S = SD
            c.S = S
        finish(c)
    return nc


def kernel(**inputs):
    P = prep_host(inputs)
    cores = list(range(8))
    nc = build_fused()
    r = run_bass_kernel_spmd(nc, [dict(
        xT=P["xT"][b], WqT=P["WqT"], WkT=P["WkT"], WvT=P["WvT"],
        bq0=P["bq0"], bk0=P["bk0"], bv0=P["bv0"], WoT=P["WoT"], bo0=P["bo0"],
        WbT=P["WbT"], bb=P["bb"], encT=P["encT"], coefs0=P["coefs0"],
        coefs1=P["coefs1"], mask0=P["mask0"], mask1=P["mask1"], Bc=P["Bc"],
        wqT=P["wqT"], wkT=P["wkT"], wvT=P["wvT"],
        bq1=P["bq1"], bk1=P["bk1"], bv1=P["bv1"],
        WdT=P["WdT"], bd=P["bd"], centT=P["centT"], asg=P["asg"],
    ) for b in range(8)], core_ids=cores)
    out = np.empty((8, T, D), np.float32)
    for b in range(8):
        out[b] = np.asarray(r.results[b]["outT"]).T
    return out


# version 09
# speedup vs baseline: 1.0895x; 1.0895x over previous
import math
import numpy as np
import ml_dtypes
from contextlib import ExitStack

import concourse.bass as bass
import concourse.mybir as mybir
from concourse.bass_utils import run_bass_kernel_spmd

F32 = mybir.dt.float32
BF16 = mybir.dt.bfloat16
I32 = mybir.dt.int32
U8 = mybir.dt.uint8
AF = mybir.ActivationFunctionType
ALU = mybir.AluOpType
AX = mybir.AxisListType

D = 4096
T = 2048
KC = 32
NT = 16
NSEG = 4
EPS = 1e-5
SQRT512 = math.sqrt(512.0)
RSQRT128 = 1.0 / math.sqrt(128.0)


def fourier_basis_f32():
    n = D
    t = np.arange(n)
    B = np.empty((n, n), np.float64)
    B[0] = 1.0 / math.sqrt(n)
    f = np.arange(1, n // 2)
    ang = 2.0 * np.pi * np.outer(f, t) / n
    B[2 * f - 1] = math.sqrt(2.0 / n) * np.cos(ang)
    B[2 * f] = math.sqrt(2.0 / n) * np.sin(ang)
    B[n - 1] = (1.0 / math.sqrt(n)) * ((-1.0) ** t)
    return np.ascontiguousarray(B.astype(np.float32))


def bcast_inner(ap, inner):
    return bass.AP(ap.tensor, ap.offset, list(ap.ap) + [[0, inner]])


class Ctx:
    def __init__(self, nc, S, block):
        self.nc, self.S, self.block = nc, S, block
        self._sems = {}
        self.n = {}
        for name in ("pe", "dv", "ac", "gp", "hd", "gd", "cc"):
            self.sem(name)

    def sem(self, name):
        if name not in self._sems:
            self._sems[name] = self.S.enter_context(self.nc.semaphore(name))
            self.n[name] = 0
        return self._sems[name]

    def bump(self, name, k=1):
        self.sem(name)
        self.n[name] += k
        return self.n[name]

    def at(self, name):
        self.sem(name)
        return self.n[name]

    def sb(self, name, shape, dtype):
        return self.S.enter_context(self.nc.sbuf_tensor(name, shape, dtype))

    def ps(self, name, shape, dtype=F32):
        return self.S.enter_context(self.nc.psum_tensor(name, shape, dtype))


def wait_tag(eng, c, tag):
    eng.wait_ge(c.sem(tag[0]), tag[1])


def emit_smalls(c):
    block = c.block
    sm = {}
    sm["identb"] = c.sb("identb", [128, 128], BF16)
    sm["identf"] = c.sb("identf", [128, 128], F32)
    colidx = c.sb("colidx", [128, 128], F32)
    partidx = c.sb("partidx", [128, 1], F32)
    sm["onescol"] = c.sb("onescol", [128, 1], BF16)
    sm["onesrow"] = c.sb("onesrow", [1, 128], F32)
    sm["eps"] = c.sb("eps", [128, 1], F32)
    sm["neg60"] = c.sb("neg60", [128, 1], F32)
    gp, dv = c.sem("gp"), c.sem("dv")

    def a0(scalar):
        from concourse.hw_specs import get_activation_tables
        for idx, (nm, funcs) in enumerate(
                get_activation_tables(c.nc.m.arch).items()):
            if AF.Exp in funcs and AF.Ln in funcs:
                scalar.add_instruction(mybir.InstLoadActFuncSet(
                    name=c.nc.get_next_instruction_name(),
                    act_func_set_id=idx, ins=[], outs=[]))
                break
    block.scalar(a0)

    def g(gpsimd):
        gpsimd.iota(colidx[:], base=0, pattern=[[1, 128]], channel_multiplier=0,
                    allow_small_or_imprecise_dtypes=True)
        gpsimd.iota(partidx[:], base=0, pattern=[[0, 1]], channel_multiplier=1,
                    allow_small_or_imprecise_dtypes=True)
        gpsimd.memset(sm["onescol"][:], 1.0)
        gpsimd.memset(sm["onesrow"][:], 1.0)
        gpsimd.memset(sm["eps"][:], EPS)
        gpsimd.memset(sm["neg60"][:], -60.0).then_inc(gp, 1)
        c.bump("gp")
    block.gpsimd(g)
    gp_t = c.at("gp")

    def v(vector):
        vector.wait_ge(gp, gp_t)
        vector.tensor_scalar(out=sm["identb"][:], in0=colidx[:],
                             scalar1=partidx[:, 0:1], scalar2=None,
                             op0=ALU.is_equal)
        vector.tensor_scalar(out=sm["identf"][:], in0=colidx[:],
                             scalar1=partidx[:, 0:1], scalar2=None,
                             op0=ALU.is_equal).then_inc(dv, 1)
        c.bump("dv")
    block.vector(v)
    sm["ready"] = ("dv", c.at("dv"))
    return sm


def emit_bias_T(c, sm, row_dram, nchunks, name, ready_tags=()):
    block = c.block
    colT = c.sb(name + "T", [128, nchunks], F32)
    _Sp = ExitStack()
    row = _Sp.enter_context(c.nc.sbuf_tensor(name + "_row",
                                             [1, nchunks * 128], F32))
    pt = _Sp.enter_context(c.nc.psum_tensor(name + "_ps", [128, nchunks], F32))
    dn = name + "_d"
    hd, pe, dv = c.sem(dn), c.sem("pe"), c.sem("dv")

    def s(sync):
        sync.wait_ge(pe, c.at("pe"))
        for rt in ready_tags:
            wait_tag(sync, c, rt)
        sync.dma_start(out=row[:], in_=row_dram[:, :]).then_inc(hd, 16)
        c.bump(dn, 16)
    block.sync(s)
    hd_t = c.at(dn)

    def t(tensor):
        tensor.wait_ge(hd, hd_t)
        wait_tag(tensor, c, sm["ready"])
        tensor.wait_ge(dv, c.at("dv"))
        last = None
        for j in range(nchunks):
            last = tensor.transpose(pt[:, j:j + 1],
                                    row[0:1, j * 128:(j + 1) * 128],
                                    sm["identf"][0:1, 0:1])
        last.then_inc(pe, 1)
        c.bump("pe")
    block.tensor(t)
    pe_t = c.at("pe")

    def v(vector):
        vector.wait_ge(pe, pe_t)
        vector.tensor_copy(colT[:], pt[:]).then_inc(dv, 1)
        c.bump("dv")
    block.vector(v)
    _Sp.close()
    return colT, ("dv", c.at("dv"))


def emit_load_stream(c, h, src_dram, cast, name="sh"):
    block = c.block
    srcr = src_dram.rearrange("(kc p) t -> p kc t", p=128)
    sem = c.sem(name)

    def g(eng):
        for i in range(4):
            sl = slice(i * KC // 4, (i + 1) * KC // 4)
            eng.dma_start(out=h[:, sl, :], in_=srcr[:, sl, :]).then_inc(sem, 16)
            c.bump(name, 16)
    (block.gpsimd if cast else block.sync)(g)
    return (name, c.at(name))



def emit_dense(c, *, WT_dram, kchunks, ochunks, moving, sink, extra_mm=None,
               tag="dn", ready_tags=()):
    block = c.block
    kh = kchunks // 4 if kchunks % 4 == 0 else (
        kchunks // 2 if kchunks % 2 == 0 else kchunks)
    nhalf = kchunks // kh
    wf32 = [c.sb(f"{tag}_f{i}", [128, kh, 128], F32) for i in range(min(nhalf, 2))]
    wcol = [c.sb(f"{tag}_w{i}", [128, kchunks, 128], BF16) for i in range(2)]
    NP = 4
    pacc = [c.ps(f"{tag}_p{i}", [128, 512], F32) for i in range(NP)]
    pe, ac = c.sem("pe"), c.sem("ac")
    fsem = [c.sem(f"{tag}_f{i}d") for i in range(min(nhalf, 2))]

    WTr = WT_dram.rearrange("(kc p) o -> p kc o", p=128)
    mm_done = {}
    conv_done = {}
    consumed = {}
    pending = []

    for oc in range(ochunks):
        b = oc % 2

        for half in range(nhalf):
            hsl = slice(half * kh, (half + 1) * kh)

            def s_w(sync):
                if oc < 2 and half == 0:
                    for rt in ready_tags:
                        wait_tag(sync, c, rt)
                hb = half % 2
                gk = oc * nhalf + half
                key = gk - 2
                if key in conv_done:
                    sync.wait_ge(ac, conv_done[key])
                sync.dma_start(
                    out=wf32[hb][:],
                    in_=WTr[:, hsl, oc * 128:(oc + 1) * 128]).then_inc(
                        fsem[hb], 16)
                c.bump(f"{tag}_f{hb}d", 16)
            block.sync(s_w)
            f_t = c.at(f"{tag}_f{half % 2}d")

            def a_w(scalar):
                scalar.wait_ge(fsem[half % 2], f_t)
                if oc >= 2 and half == 0:
                    scalar.wait_ge(pe, mm_done[oc - 2])
                scalar.copy(wcol[b][:, hsl, :],
                            wf32[half % 2][:]).then_inc(ac, 1)
                conv_done[oc * nhalf + half] = c.bump("ac")
            block.scalar(a_w)
        cv_t = conv_done[oc * nhalf + nhalf - 1]

        for tseg in range(NSEG):
            uses = oc * NSEG + tseg
            pidx = uses % NP

            def t(tensor):
                if tseg == 0:
                    tensor.wait_ge(ac, cv_t)
                if uses >= NP:
                    wait_tag(tensor, c, consumed[uses - NP])
                last = None
                for dc in range(kchunks):
                    last = tensor.matmul(
                        pacc[pidx][:, :],
                        wcol[b][:, dc, :],
                        moving[:, dc, tseg * 512:(tseg + 1) * 512],
                        start=(dc == 0), stop=(dc == kchunks - 1
                                               and extra_mm is None),
                        skip_group_check=True)
                if extra_mm is not None:
                    last = extra_mm(tensor, oc, tseg, pacc[pidx])
                last.then_inc(pe, 1)
                mm_done[oc] = c.bump("pe")
            block.tensor(t)

            if pending:
                o2, s2 = pending.pop(0)
                sink.post(c, o2, s2)
                if s2 == NSEG - 1:
                    sink.oc_done(c, o2)
            consumed[uses] = sink.evac(c, oc, tseg, pacc[pidx])
            pending.append((oc, tseg))
    while pending:
        o2, s2 = pending.pop(0)
        sink.post(c, o2, s2)
        if s2 == NSEG - 1:
            sink.oc_done(c, o2)
    sink.finish(c)


class BaseSink:
    def post(self, c, oc, tseg):
        pass

    def oc_done(self, c, oc):
        pass

    def finish(self, c):
        pass



class LNSink(BaseSink):
    def __init__(self, c, sm, tag, resid, bias_colT, s_scr, psstat, mask=None):
        self.sm, self.tag = sm, tag
        self.resid = resid
        self.bias_colT = bias_colT
        self.mask = mask
        self.schunk = c.sb(tag + "_sc", [128, 2, T], BF16)
        self.sq = c.sb(tag + "_sq", [128, 2, 512], BF16)
        if mask is not None:
            self.mtmp = c.sb(tag + "_mt", [128, 512], BF16)
        self.psstat = psstat
        self.first = True
        self.scrr = s_scr.rearrange("(kc p) t -> p kc t", p=128)
        self.ssem = [c.sem(tag + "_s0d"), c.sem(tag + "_s1d")]
        self.snames = [tag + "_s0d", tag + "_s1d"]
        self.evac_cnt = {}
        self.sq_cnt = {}
        self.stat_cnt = {}
        self.store_cnt = {}

    def evac(self, c, oc, tseg, psum):
        block, sm = c.block, self.sm
        pe, dv, ac = (c.sem(x) for x in ("pe", "dv", "ac"))
        pe_t = c.at("pe")
        sb = self.schunk[:, oc % 2, :]
        seg = slice(tseg * 512, (tseg + 1) * 512)

        def v(vector):
            vector.wait_ge(pe, pe_t)
            if oc >= 2:
                if tseg == 0 and (oc - 2) in self.store_cnt:
                    vector.wait_ge(self.ssem[oc % 2], self.store_cnt[oc - 2])
                if (oc - 2, tseg) in self.stat_cnt:
                    vector.wait_ge(pe, self.stat_cnt[(oc - 2, tseg)])
            if self.mask is not None:
                rcolT, qB = self.mask
                vector.wait_ge(dv, c.at("dv"))
                vector.tensor_scalar(
                    out=self.mtmp[:], in0=psum[:, :],
                    scalar1=self.bias_colT[:, oc:oc + 1],
                    scalar2=rcolT[:, oc:oc + 1],
                    op0=ALU.add, op1=ALU.mult).then_inc(dv, 1)
                c.bump("dv")
                vector.wait_ge(dv, c.at("dv"))
                vector.scalar_tensor_tensor(
                    out=self.mtmp[:], in0=self.mtmp[:], scalar=1.0,
                    in1=qB[:, seg], op0=ALU.mult, op1=ALU.mult).then_inc(dv, 1)
                c.bump("dv")
                vector.wait_ge(dv, c.at("dv"))
                vector.tensor_tensor(
                    out=sb[:, seg], in0=self.mtmp[:],
                    in1=self.resid[:, oc, seg], op=ALU.add).then_inc(dv, 1)
                c.bump("dv")
            elif self.bias_colT is not None and self.resid is not None:
                vector.scalar_tensor_tensor(
                    out=sb[:, seg], in0=psum[:, :],
                    scalar=self.bias_colT[:, oc:oc + 1],
                    in1=self.resid[:, oc, seg],
                    op0=ALU.add, op1=ALU.add).then_inc(dv, 1)
                c.bump("dv")
            elif self.bias_colT is not None:
                vector.tensor_scalar(
                    out=sb[:, seg], in0=psum[:, :],
                    scalar1=self.bias_colT[:, oc:oc + 1], scalar2=None,
                    op0=ALU.add).then_inc(dv, 1)
                c.bump("dv")
            else:
                vector.tensor_tensor(
                    out=sb[:, seg], in0=psum[:, :],
                    in1=self.resid[:, oc, seg], op=ALU.add).then_inc(dv, 1)
                c.bump("dv")
        block.vector(v)
        self.evac_cnt[(oc, tseg)] = c.at("dv")
        dv_t = c.at("dv")
        psum_tag = ("dv", dv_t if self.mask is None else dv_t - 1)

        def a(vector):
            vector.wait_ge(dv, dv_t)
            prev = (oc, tseg - 1) if tseg > 0 else (oc - 2, NSEG - 1)
            if prev in self.stat_cnt:
                vector.wait_ge(pe, self.stat_cnt[prev])
            vector.tensor_tensor(out=self.sq[:, oc % 2, :], in0=sb[:, seg],
                                 in1=sb[:, seg], op=ALU.mult).then_inc(dv, 1)
            c.bump("dv")
        block.vector(a)
        self.sq_cnt[(oc, tseg)] = ("dv", c.at("dv"))
        return psum_tag

    def post(self, c, oc, tseg):
        block, sm = c.block, self.sm
        pe, dv, ac = c.sem("pe"), c.sem("dv"), c.sem("ac")
        sb = self.schunk[:, oc % 2, :]
        dv_t = self.evac_cnt[(oc, tseg)]
        ac_t = self.sq_cnt[(oc, tseg)]

        def t(tensor):
            tensor.wait_ge(dv, dv_t)
            wait_tag(tensor, c, ac_t)
            last = None
            for tb in range(4):
                tt = tseg * 4 + tb
                blk = slice(tseg * 512 + tb * 128, tseg * 512 + (tb + 1) * 128)
                tensor.matmul(self.psstat[:, tt:tt + 1], sb[:, blk],
                              sm["onescol"][:, :],
                              start=self.first, stop=False,
                              skip_group_check=True)
                self.first = False
                last = tensor.matmul(self.psstat[:, 16 + tt:17 + tt],
                                     self.sq[:, oc % 2,
                                             tb * 128:(tb + 1) * 128],
                                     sm["onescol"][:, :],
                                     start=False, stop=False,
                                     skip_group_check=True)
            last.then_inc(pe, 1)
            self.stat_cnt[(oc, tseg)] = c.bump("pe")
        block.tensor(t)

    def oc_done(self, c, oc):
        block = c.block
        dv = c.sem("dv")
        dv_t = self.evac_cnt[(oc, NSEG - 1)]

        def s(sync):
            sync.wait_ge(dv, dv_t)
            sync.dma_start(out=self.scrr[:, oc, :],
                           in_=self.schunk[:, oc % 2, :]).then_inc(
                               self.ssem[oc % 2], 16)
            self.store_cnt[oc] = c.bump(self.snames[oc % 2], 16)
        block.sync(s)

    def store_tags(self, c):
        return [(n, c.at(n)) for n in self.snames]



def emit_normalize(c, sm, tag, s_scr, h, psstat, flat_scr, dest_mode="h",
                   out_dram=None, store_tags=()):
    block = c.block
    pe, dv, ac = (c.sem(x) for x in ("pe", "dv", "ac"))
    fsem = c.sem(tag + "_fd")
    lsem = [c.sem(tag + "_l0d"), c.sem(tag + "_l1d")]
    lnames = [tag + "_l0d", tag + "_l1d"]
    osem = c.sem(tag + "_od")

    mean = c.sb(tag + "_mean", [128, 16], F32)
    var = c.sb(tag + "_var", [128, 16], F32)
    rstd = c.sb(tag + "_rstd", [128, 16], F32)
    bneg = c.sb(tag + "_bneg", [128, 16], F32)
    tmp16 = c.sb(tag + "_tmp16", [128, 16], F32)
    rows_sb = c.sb(tag + "_rows", [16, 256], F32)
    rows2 = c.sb(tag + "_rows2", [1, 2 * T], F32)
    aB = c.sb(tag + "_aB", [128, T], BF16)
    bB = c.sb(tag + "_bB", [128, T], BF16)
    ptr = c.ps(tag + "_ptr", [16, 256], F32)
    pbb = c.ps(tag + "_pbb", [128, 512], F32)

    last_stat = c.at("pe")

    def v1(vector):
        vector.wait_ge(pe, last_stat)
        for t_ in store_tags:
            wait_tag(vector, c, t_)
        vector.tensor_scalar(out=mean[:], in0=psstat[:, 0:16],
                             scalar1=1.0 / D, scalar2=None, op0=ALU.mult)
        vector.tensor_scalar(out=var[:], in0=psstat[:, 16:32],
                             scalar1=1.0 / D, scalar2=None,
                             op0=ALU.mult).then_inc(dv, 1)
        c.bump("dv")
        vector.wait_ge(dv, c.at("dv"))
        vector.tensor_tensor(out=tmp16[:], in0=mean[:], in1=mean[:],
                             op=ALU.mult).then_inc(dv, 1)
        c.bump("dv")
        vector.wait_ge(dv, c.at("dv"))
        vector.tensor_tensor(out=var[:], in0=var[:], in1=tmp16[:],
                             op=ALU.subtract).then_inc(dv, 1)
        c.bump("dv")
    block.vector(v1)
    dv_t = c.at("dv")

    def a1(scalar):
        scalar.wait_ge(dv, dv_t)
        scalar.activation(out=var[:], in_=var[:], func=AF.Ln,
                          bias=sm["eps"][:, 0:1], scale=1.0).then_inc(ac, 1)
        c.bump("ac")
        scalar.wait_ge(ac, c.at("ac"))
        scalar.activation(out=rstd[:], in_=var[:], func=AF.Exp,
                          scale=-0.5).then_inc(ac, 1)
        c.bump("ac")
    block.scalar(a1)
    ac_t = c.at("ac")

    def v2(vector):
        vector.wait_ge(ac, ac_t)
        vector.tensor_scalar(out=tmp16[:], in0=mean[:], scalar1=-1.0,
                             scalar2=None, op0=ALU.mult).then_inc(dv, 1)
        c.bump("dv")
        vector.wait_ge(dv, c.at("dv"))
        vector.tensor_tensor(out=bneg[:], in0=tmp16[:], in1=rstd[:],
                             op=ALU.mult).then_inc(dv, 1)
        c.bump("dv")
    block.vector(v2)
    dv_t2 = c.at("dv")

    def t1(tensor):
        tensor.wait_ge(dv, dv_t2)
        tensor.transpose(ptr[0:16, 0:128], rstd[:], sm["identf"][:])
        tensor.transpose(ptr[0:16, 128:256], bneg[:],
                         sm["identf"][:]).then_inc(pe, 1)
        c.bump("pe")
    block.tensor(t1)
    pe_t = c.at("pe")

    def v3(vector):
        vector.wait_ge(pe, pe_t)
        vector.tensor_copy(rows_sb[:, :], ptr[0:16, :]).then_inc(dv, 1)
        c.bump("dv")
    block.vector(v3)
    dv_t3 = c.at("dv")

    def g1(gpsimd):
        gpsimd.wait_ge(dv, dv_t3)
        gpsimd.dma_start(out=flat_scr[:, :], in_=rows_sb[:, :]).then_inc(fsem, 16)
        c.bump(tag + "_fd", 16)
        gpsimd.wait_ge(fsem, c.at(tag + "_fd"))
        a_src = flat_scr.rearrange("p (h i) -> h p i", h=2)
        r2v = rows2.rearrange("a (h p i) -> a h p i", h=2, p=16)
        for r in range(2):
            gpsimd.dma_start(out=r2v[:, r, :, :],
                             in_=a_src[r:r + 1, :, :]).then_inc(fsem, 16)
            c.bump(tag + "_fd", 16)
    block.gpsimd(g1)
    gd_t = c.at(tag + "_fd")

    for r in range(2):
        for seg in range(NSEG):
            def tmm(tensor):
                if r == 0 and seg == 0:
                    tensor.wait_ge(fsem, gd_t)
                else:
                    tensor.wait_ge(dv, c.at("dv"))
                tensor.matmul(pbb[:, :], sm["onesrow"][:, :],
                              rows2[0:1, r * T + seg * 512:
                                    r * T + (seg + 1) * 512],
                              start=True, stop=True).then_inc(pe, 1)
                c.bump("pe")
            block.tensor(tmm)
            pe_b = c.at("pe")

            def vc(vector):
                vector.wait_ge(pe, pe_b)
                dst = aB if r == 0 else bB
                vector.tensor_copy(dst[:, seg * 512:(seg + 1) * 512],
                                   pbb[:, :]).then_inc(dv, 1)
                c.bump("dv")
            block.vector(vc)
    rowsB_t = c.at("dv")

    sbuf_in = c.sb(tag + "_nin", [128, 2, T], BF16)
    ntmp = c.sb(tag + "_ntmp", [128, T], BF16)
    nout = c.sb(tag + "_nout", [128, T], F32) if dest_mode == "out" else None
    scrr = s_scr.rearrange("(kc p) t -> p kc t", p=128)
    outr = (out_dram.rearrange("(kc p) t -> p kc t", p=128)
            if out_dram is not None else None)
    ld_cnt, nm_cnt, st_cnt = {}, {}, {}
    hd0 = c.at("hd")

    for oc in range(KC):
        def s1(sync):
            if oc == 0:
                sync.wait_ge(pe, last_stat)
                for t_ in store_tags:
                    wait_tag(sync, c, t_)
            if oc >= 2:
                sync.wait_ge(dv, nm_cnt[oc - 2])
            sync.dma_start(out=sbuf_in[:, oc % 2, :],
                           in_=scrr[:, oc, :]).then_inc(lsem[oc % 2], 16)
            ld_cnt[oc] = c.bump(lnames[oc % 2], 16)
        block.sync(s1)

        def v4(vector):
            vector.wait_ge(lsem[oc % 2], ld_cnt[oc])
            vector.wait_ge(dv, c.at("dv"))
            if oc == 0:
                vector.wait_ge(dv, rowsB_t)
            if dest_mode == "out" and oc >= 1:
                vector.wait_ge(osem, st_cnt[oc - 1])
            vector.tensor_tensor(out=ntmp[:], in0=sbuf_in[:, oc % 2, :],
                                 in1=aB[:], op=ALU.mult).then_inc(dv, 1)
            c.bump("dv")
            vector.wait_ge(dv, c.at("dv"))
            dst = nout[:] if dest_mode == "out" else h[:, oc, :]
            vector.tensor_tensor(out=dst, in0=ntmp[:], in1=bB[:],
                                 op=ALU.add).then_inc(dv, 1)
            nm_cnt[oc] = c.bump("dv")
        block.vector(v4)

        if dest_mode == "out":
            def s2(sync):
                sync.wait_ge(dv, nm_cnt[oc])
                sync.dma_start(out=outr[:, oc, :], in_=nout[:]).then_inc(osem, 16)
                st_cnt[oc] = c.bump(tag + "_od", 16)
            block.sync(s2)
        elif dest_mode == "both":
            def s2(sync):
                sync.wait_ge(dv, nm_cnt[oc])
                sync.dma_start(out=outr[:, oc, :],
                               in_=h[:, oc, :]).then_inc(osem, 16)
                st_cnt[oc] = c.bump(tag + "_od", 16)
            block.sync(s2)
    if dest_mode == "h":
        return ("dv", c.at("dv"))
    return (tag + "_od", c.at(tag + "_od"))



class QKSink(BaseSink):

    def __init__(self, c, tag, bias_colT, scr):
        self.tag = tag
        self.bias_colT = bias_colT
        self.scr = scr
        self.chunk = c.sb(tag + "_ck", [128, 2, T], BF16)
        self.ssem = [c.sem(tag + "_s0d"), c.sem(tag + "_s1d")]
        self.snames = [tag + "_s0d", tag + "_s1d"]
        self.evac_cnt = {}
        self.store_cnt = {}

    def store_tags(self, c):
        return [(n, c.at(n)) for n in self.snames]

    def evac(self, c, oc, tseg, psum):
        block = c.block
        pe, dv = c.sem("pe"), c.sem("dv")
        pe_t = c.at("pe")
        sb = self.chunk[:, oc % 2, tseg * 512:(tseg + 1) * 512]

        def v(vector):
            vector.wait_ge(pe, pe_t)
            if tseg == 0 and (oc - 2) in self.store_cnt:
                vector.wait_ge(self.ssem[oc % 2], self.store_cnt[oc - 2])
            vector.tensor_scalar(out=sb, in0=psum[:, :],
                                 scalar1=self.bias_colT[:, oc:oc + 1],
                                 scalar2=None, op0=ALU.add).then_inc(dv, 1)
            self.evac_cnt[(oc, tseg)] = c.bump("dv")
        block.vector(v)
        return ("dv", self.evac_cnt[(oc, tseg)])

    def oc_done(self, c, oc):
        block = c.block
        dv = c.sem("dv")
        dv_t = self.evac_cnt[(oc, NSEG - 1)]

        def s(sync):
            sync.wait_ge(dv, dv_t)
            sync.dma_start(out=self.scr[oc * 128:(oc + 1) * 128, :],
                           in_=self.chunk[:, oc % 2, :]).then_inc(
                               self.ssem[oc % 2], 16)
            self.store_cnt[oc] = c.bump(self.snames[oc % 2], 16)
        block.sync(s)


class VSink(BaseSink):

    def __init__(self, c, sm, tag, bias_colT, scr):
        self.sm = sm
        self.tag = tag
        self.bias_colT = bias_colT
        self.scr = scr
        self.vtmp = c.sb(tag + "_vt", [128, 2, 512], BF16)
        self.ttmp = c.sb(tag + "_tt", [128, 2, 512], BF16)
        self.pvt = c.ps(tag + "_pv", [128, 512], BF16)
        self.ssem = [c.sem(tag + "_s0d"), c.sem(tag + "_s1d")]
        self.snames = [tag + "_s0d", tag + "_s1d"]
        self.evac_cnt = {}
        self.tr_cnt = {}
        self.ev2_cnt = {}
        self.store_cnt = {}

    def store_tags(self, c):
        return [(n, c.at(n)) for n in self.snames]

    def evac(self, c, oc, tseg, psum):
        block = c.block
        pe, dv, ac = (c.sem(x) for x in ("pe", "dv", "ac"))
        pe_t = c.at("pe")
        u = (oc * NSEG + tseg) % 2
        sb = self.vtmp[:, u, :]

        def v(vector):
            vector.wait_ge(pe, pe_t)
            key = (oc * NSEG + tseg) - 2
            if key >= 0:
                vector.wait_ge(pe, self.tr_cnt[key])
            vector.tensor_scalar(out=sb, in0=psum[:, :],
                                 scalar1=self.bias_colT[:, oc:oc + 1],
                                 scalar2=None, op0=ALU.add).then_inc(dv, 1)
            self.evac_cnt[(oc, tseg)] = c.bump("dv")
        block.vector(v)
        dv_t = self.evac_cnt[(oc, tseg)]

        def t(tensor):
            tensor.wait_ge(dv, dv_t)
            key = (oc * NSEG + tseg) - 1
            if key >= 0:
                tensor.wait_ge(dv, self.ev2_cnt[key])
            last = None
            for j in range(4):
                last = tensor.transpose(self.pvt[:, j * 128:(j + 1) * 128],
                                        sb[:, j * 128:(j + 1) * 128],
                                        self.sm["identb"][:])
            last.then_inc(pe, 1)
            self.tr_cnt[oc * NSEG + tseg] = c.bump("pe")
        block.tensor(t)
        pe_t2 = c.at("pe")

        def a(vector):
            vector.wait_ge(pe, pe_t2)
            key = (oc * NSEG + tseg) - 2
            if key >= 0:
                vector.wait_ge(self.ssem[u], self.store_cnt[key])
            vector.tensor_copy(self.ttmp[:, u, :], self.pvt[:, :]).then_inc(dv, 1)
            self.ev2_cnt[oc * NSEG + tseg] = c.bump("dv")
        block.vector(a)
        ac_t = c.at("dv")

        def s(sync):
            sync.wait_ge(dv, ac_t)
            tv = self.ttmp[:, u, :].rearrange("p (j e) -> p j e", j=4)
            dst = self.scr.rearrange("(tb p) e -> p tb e", p=128)
            sync.dma_start(
                out=dst[:, tseg * 4:(tseg + 1) * 4,
                        oc * 128:(oc + 1) * 128],
                in_=tv[:, :, :]).then_inc(self.ssem[u], 16)
            self.store_cnt[oc * NSEG + tseg] = c.bump(self.snames[u], 16)
        block.sync(s)
        return ("dv", dv_t)



class Res:
    def __init__(self):
        self.tags = []

    def used(self, tag):
        self.tags.append(tag)

    def wait(self, eng, c):
        for t in self.tags:
            wait_tag(eng, c, t)
        self.tags = []



def emit_l0_attention(c, sm, scr_q, scr_k, scr_v, ctxT, ready_tags=()):
    block = c.block
    pe, dv, ac = (c.sem(x) for x in ("pe", "dv", "ac"))
    hsem = [c.sem("att_h0d"), c.sem("att_h1d")]
    hnames = ["att_h0d", "att_h1d"]

    qh = c.sb("att_qh", [64, 2, T], BF16)
    kh = c.sb("att_kh", [64, 2, T], BF16)
    vh = c.sb("att_vh", [128, 2, NT, 64], BF16)
    P = c.sb("att_P", [128, 2, T], BF16)
    PT = c.sb("att_PT", [128, 2, NT, 128], BF16)
    negmax = c.sb("att_nm", [128, 2], F32)
    rowsum = c.sb("att_rs", [128, 2], F32)
    rr = c.sb("att_rr", [128, 2], F32)
    pS = c.ps("att_pS", [128, T], F32)
    pPT = c.ps("att_pPT", [128, T], BF16)
    pC = c.ps("att_pC", [128, 512], F32)

    scr_vr = scr_v.rearrange("(tb p) e -> p tb e", p=128)
    hload = {}
    r_hbuf = [Res(), Res()]
    r_pS = Res()
    r_P = [Res(), Res()]
    r_pPT = Res()
    r_PT = [Res(), Res()]
    r_pC = [Res(), Res(), Res(), Res()]
    r_ctxT = Res()

    def emit_front(h, hb, qt):
        u = qt % 2
        qsl = slice(qt * 128, (qt + 1) * 128)

        def t_s(tensor):
            if qt == 0:
                tensor.wait_ge(hsem[hb], hload[h])
            r_pS.wait(tensor, c)
            last = None
            for kseg in range(NSEG):
                last = tensor.matmul(pS[:, kseg * 512:(kseg + 1) * 512],
                                     qh[:, hb, qsl],
                                     kh[:, hb, kseg * 512:(kseg + 1) * 512],
                                     start=True, stop=True)
            last.then_inc(pe, 1)
            c.bump("pe")
        block.tensor(t_s)
        pe_t = c.at("pe")

        def a_e(scalar):
            scalar.wait_ge(pe, pe_t)
            r_P[u].wait(scalar, c)
            scalar.activation(out=P[:, u, :], in_=pS[:, :], func=AF.Exp,
                              bias=sm["neg60"][:, 0:1], scale=1.0,
                              accum_out=rowsum[:, u:u + 1]).then_inc(ac, 1)
            c.bump("ac")
        block.scalar(a_e)
        r_pS.used(("ac", c.at("ac")))
        ac_t = c.at("ac")

        def v_r(vector):
            vector.wait_ge(ac, ac_t)
            vector.reciprocal(rr[:, u:u + 1], rowsum[:, u:u + 1]).then_inc(dv, 1)
            c.bump("dv")
            vector.wait_ge(dv, c.at("dv"))
            vector.tensor_scalar(out=P[:, u, :], in0=P[:, u, :],
                                 scalar1=rr[:, u:u + 1], scalar2=1.0 / SQRT512,
                                 op0=ALU.mult, op1=ALU.mult).then_inc(dv, 1)
            c.bump("dv")
        block.vector(v_r)
        return c.at("dv")

    def emit_back(h, hb, qt, dv_t2):
        u = qt % 2
        qsl = slice(qt * 128, (qt + 1) * 128)

        def t_pt(tensor):
            tensor.wait_ge(dv, dv_t2)
            r_pPT.wait(tensor, c)
            last = None
            for kt in range(NT):
                last = tensor.transpose(pPT[:, kt * 128:(kt + 1) * 128],
                                        P[:, u, kt * 128:(kt + 1) * 128],
                                        sm["identb"][:])
            last.then_inc(pe, 1)
            c.bump("pe")
        block.tensor(t_pt)
        r_P[u].used(("pe", c.at("pe")))
        pe_t2 = c.at("pe")

        def a_pt(vector):
            vector.wait_ge(pe, pe_t2)
            r_PT[u].wait(vector, c)
            vector.tensor_copy(PT[:, u, :, :], pPT[:, :]).then_inc(dv, 1)
            c.bump("dv")
        block.vector(a_pt)
        r_pPT.used(("dv", c.at("dv")))
        ac_t2 = c.at("dv")

        def t_c(tensor):
            tensor.wait_ge(dv, ac_t2)
            r_pC[qt % 4].wait(tensor, c)
            tp = (0, 64) if (h % 2) else None
            outap = pC[(h % 2) * 64:(h % 2) * 64 + 64,
                       (qt % 4) * 128:(qt % 4) * 128 + 128]
            last = None
            for kt in range(NT):
                last = tensor.matmul(outap, vh[:, hb, kt, :],
                                     PT[:, u, kt, :],
                                     start=(kt == 0), stop=(kt == NT - 1),
                                     skip_group_check=True,
                                     tile_position=tp)
            last.then_inc(pe, 1)
            c.bump("pe")
        block.tensor(t_c)
        tagc = ("pe", c.at("pe"))
        r_PT[u].used(tagc)
        if qt == NT - 1:
            r_hbuf[hb].used(tagc)
        pe_t3 = c.at("pe")

        def v_c(vector):
            vector.wait_ge(pe, pe_t3)
            vector.tensor_copy(
                ctxT[(h % 2) * 64:(h % 2) * 64 + 64, h // 2, qsl],
                pC[(h % 2) * 64:(h % 2) * 64 + 64,
                   (qt % 4) * 128:(qt % 4) * 128 + 128]).then_inc(dv, 1)
            c.bump("dv")
        block.vector(v_c)
        r_pC[qt % 4].used(("dv", c.at("dv")))

    pending = []
    for h in range(8):
        hb = h % 2

        def s(sync):
            if h == 0:
                for rt in ready_tags:
                    wait_tag(sync, c, rt)
            r_hbuf[hb].wait(sync, c)
            sync.dma_start(out=qh[:, hb, :],
                           in_=scr_q[h * 64:(h + 1) * 64, :]).then_inc(hsem[hb], 16)
            c.bump(hnames[hb], 16)
            sync.dma_start(out=kh[:, hb, :],
                           in_=scr_k[h * 64:(h + 1) * 64, :]).then_inc(hsem[hb], 16)
            c.bump(hnames[hb], 16)
            sync.dma_start(out=vh[:, hb, :, :],
                           in_=scr_vr[:, :, h * 64:(h + 1) * 64]).then_inc(hsem[hb], 16)
            hload[h] = c.bump(hnames[hb], 16)
        block.sync(s)

        for qt in range(NT):
            dv_t2 = emit_front(h, hb, qt)
            pending.append((h, hb, qt, dv_t2))
            if len(pending) >= 2:
                emit_back(*pending.pop(0))
    while pending:
        emit_back(*pending.pop(0))
    return ("dv", c.at("dv"))



def emit_G_setup(c, sm, coefs0_d, mask0_d, coefs1_d, mask1_d, B_d, GT,
                 ready_tags=()):
    block = c.block
    pe, dv, ac = c.sem("pe"), c.sem("dv"), c.sem("ac")

    cbuf = c.sb("G_c", [128, 2, 3, 40], F32)
    mbuf = c.sb("G_m", [128, 2, 3, 40], U8)
    mf = c.sb("G_mf", [128, 3, 40], F32)
    W01b = c.sb("G_w01", [128, KC, 40], BF16)
    Bf32 = [c.sb(f"G_Bf{i}", [128, 2048], F32) for i in range(2)]
    Bbuf = c.sb("G_B", [128, 2, D], BF16)
    psG = c.ps("G_ps", [40, D], F32)

    c0r = coefs0_d.rearrange("e (oc p) r -> p oc e r", p=128)
    m0r = mask0_d.rearrange("e (oc p) r -> p oc e r", p=128)
    c1r = coefs1_d.rearrange("e (oc p) r -> p oc e r", p=128)
    m1r = mask1_d.rearrange("e (oc p) r -> p oc e r", p=128)

    csem = [c.sem("G_c0d"), c.sem("G_c1d")]
    cnames = ["G_c0d", "G_c1d"]
    r_cbuf = [Res(), Res()]
    for oc in range(KC):
        b = oc % 2

        def g(sync):
            if oc == 0:
                for rt in ready_tags:
                    wait_tag(sync, c, rt)
            r_cbuf[b].wait(sync, c)
            sync.dma_start(out=cbuf[:, b, :, 0:32],
                           in_=c0r[:, oc, :, :]).then_inc(csem[b], 16)
            c.bump(cnames[b], 16)
            sync.dma_start(out=cbuf[:, b, :, 32:40],
                           in_=c1r[:, oc, :, :]).then_inc(csem[b], 16)
            c.bump(cnames[b], 16)
            sync.dma_start(out=mbuf[:, b, :, 0:32],
                           in_=m0r[:, oc, :, :]).then_inc(csem[b], 16)
            c.bump(cnames[b], 16)
            sync.dma_start(out=mbuf[:, b, :, 32:40],
                           in_=m1r[:, oc, :, :]).then_inc(csem[b], 16)
            c.bump(cnames[b], 16)
        block.sync(g)
        gd_t = c.at(cnames[b])

        def v(vector):
            vector.wait_ge(csem[b], gd_t)
            vector.wait_ge(dv, c.at("dv"))
            vector.tensor_copy(mf[:], mbuf[:, b, :, :]).then_inc(dv, 1)
            c.bump("dv")
            vector.wait_ge(dv, c.at("dv"))
            vector.tensor_tensor(out=mf[:], in0=mf[:], in1=cbuf[:, b, :, :],
                                 op=ALU.mult).then_inc(dv, 1)
            c.bump("dv")
            vector.wait_ge(dv, c.at("dv"))
            vector.tensor_tensor(out=mf[:, 0, :], in0=mf[:, 0, :],
                                 in1=mf[:, 1, :], op=ALU.add).then_inc(dv, 1)
            c.bump("dv")
            vector.wait_ge(dv, c.at("dv"))
            vector.tensor_tensor(out=W01b[:, oc, :], in0=mf[:, 0, :],
                                 in1=mf[:, 2, :], op=ALU.add).then_inc(dv, 1)
            c.bump("dv")
        block.vector(v)
        r_cbuf[b].used(("dv", c.at("dv")))
    w01_t = c.at("dv")

    Br = B_d.rearrange("(jc p) t -> p jc t", p=128)
    bsem = [c.sem("G_b0d"), c.sem("G_b1d")]
    bnames = ["G_b0d", "G_b1d"]
    conv_done = {}
    mm_t = {}
    for jc in range(KC):
        b = jc % 2
        for half in range(2):
            def g2(sync):
                key = (jc - 1, half)
                if key in conv_done:
                    sync.wait_ge(ac, conv_done[key])
                sync.dma_start(
                    out=Bf32[half][:],
                    in_=Br[:, jc, half * 2048:(half + 1) * 2048]).then_inc(
                        bsem[half], 16)
                c.bump(bnames[half], 16)
            block.sync(g2)
            b_t = c.at(bnames[half])

            def a2(scalar):
                scalar.wait_ge(bsem[half], b_t)
                if jc >= 2 and half == 0:
                    scalar.wait_ge(pe, mm_t[jc - 2])
                scalar.copy(Bbuf[:, b, half * 2048:(half + 1) * 2048],
                            Bf32[half][:]).then_inc(ac, 1)
                conv_done[(jc, half)] = c.bump("ac")
            block.scalar(a2)
        cv_t = conv_done[(jc, 1)]

        def t(tensor):
            tensor.wait_ge(ac, cv_t)
            if jc == 0:
                tensor.wait_ge(dv, w01_t)
            last = None
            for seg in range(8):
                last = tensor.matmul(psG[0:40, seg * 512:(seg + 1) * 512],
                                     W01b[:, jc, :],
                                     Bbuf[:, b, seg * 512:(seg + 1) * 512],
                                     start=(jc == 0), stop=(jc == KC - 1),
                                     skip_group_check=True)
            last.then_inc(pe, 1)
            mm_t[jc] = c.bump("pe")
        block.tensor(t)
    pe_t = c.at("pe")

    def v2(vector):
        vector.wait_ge(pe, pe_t)
        vector.tensor_scalar(out=GT[0:40, :], in0=psG[0:40, :],
                             scalar1=0.5, scalar2=None,
                             op0=ALU.mult).then_inc(dv, 1)
        c.bump("dv")
    block.vector(v2)
    return ("dv", c.at("dv"))


def emit_eT(c, sm, encT_sb, h, eT, ready_tags=()):
    block = c.block
    pe, dv = c.sem("pe"), c.sem("dv")
    psE = c.ps("eT_ps", [40, T], F32)

    def t(tensor):
        for rt in ready_tags:
            wait_tag(tensor, c, rt)
        last = None
        for dc in range(KC):
            for seg in range(NSEG):
                last = tensor.matmul(psE[0:40, seg * 512:(seg + 1) * 512],
                                     encT_sb[:, dc, :],
                                     h[:, dc, seg * 512:(seg + 1) * 512],
                                     start=(dc == 0), stop=(dc == KC - 1),
                                     skip_group_check=True)
        last.then_inc(pe, 1)
        c.bump("pe")
    block.tensor(t)
    pe_t = c.at("pe")

    def v(vector):
        vector.wait_ge(pe, pe_t)
        vector.tensor_copy(eT[0:40, :], psE[0:40, :]).then_inc(dv, 1)
        c.bump("dv")
    block.vector(v)
    return ("dv", c.at("dv"))


def emit_l1mix(c, sm, scr_q1, scr_k1, v1tok, h, s_scr, psstat,
               ready_tags=()):
    block = c.block
    pe, dv, ac = (c.sem(x) for x in ("pe", "dv", "ac"))
    isem = c.sem("mx_id")
    s3sem = [c.sem("mx_s0d"), c.sem("mx_s1d")]
    s3names = ["mx_s0d", "mx_s1d"]

    qTt = c.sb("mx_q", [128, 1, KC, 128], BF16)
    kTt = c.sb("mx_k", [128, 1, KC, 128], BF16)
    vgrp = c.sb("mx_v", [32, 2, 16, 128], BF16)
    E = c.sb("mx_E", [32, 2, 512], F32)
    esum = c.sb("mx_es", [32, 2, 16], F32)
    Pn = c.sb("mx_Pn", [32, 2, 512], BF16)
    sbPT = c.sb("mx_PT", [32, 16, 32], BF16)
    s3blk = c.sb("mx_s3", [128, 2, KC, 128], BF16)
    sq3 = c.sb("mx_sq", [128, 1, KC, 128], BF16)
    pM = [c.ps("mx_pM0", [32, 512], F32), c.ps("mx_pM1", [32, 512], F32)]
    pPT2 = c.ps("mx_pPT", [32, 512], BF16)
    pC2 = c.ps("mx_pC", [128, 512], F32)

    qr = scr_q1.rearrange("(hc p) t -> p hc t", p=128)
    kr = scr_k1.rearrange("(hc p) t -> p hc t", p=128)
    vr = v1tok.rearrange("(tb t) (hp d) -> hp tb t d", t=128, d=128)
    s3r = s_scr.rearrange("(hc p) t -> p hc t", p=128)

    r_in = [Res()]
    r_vg = [Res(), Res()]
    vsem = [c.sem("mx_v0d"), c.sem("mx_v1d")]
    vnames = ["mx_v0d", "mx_v1d"]
    r_pM = [Res(), Res()]
    r_E = [Res(), Res()]
    r_Pn = [Res(), Res()]
    mpend = []
    r_pPT2 = Res()
    r_sbT = Res()
    r_pC2 = Res()
    r_s3 = [Res(), Res()]
    prev_stat = [0]
    stat_first = [True]

    for tt in range(NT):
        b3 = tt % 2
        b = 0

        def s(sync):
            if tt == 0:
                for rt in ready_tags:
                    wait_tag(sync, c, rt)
            r_in[b].wait(sync, c)
            tsl = slice(tt * 128, (tt + 1) * 128)
            sync.dma_start(out=qTt[:, b, :, :], in_=qr[:, :, tsl]).then_inc(isem, 16)
            c.bump("mx_id", 16)
            sync.dma_start(out=kTt[:, b, :, :], in_=kr[:, :, tsl]).then_inc(isem, 16)
            c.bump("mx_id", 16)
        block.sync(s)
        hd_t = c.at("mx_id")

        def mix_front(grp, gb):
            def s_v(sync):
                r_vg[gb].wait(sync, c)
                sync.dma_start(
                    out=vgrp[:, gb, :, :],
                    in_=vr[:, tt, grp * 16:(grp + 1) * 16, :]).then_inc(
                        vsem[gb], 16)
                c.bump(vnames[gb], 16)
            block.sync(s_v)
            v_t = c.at(vnames[gb])

            def t_s(tensor):
                if grp == 0:
                    tensor.wait_ge(isem, hd_t)
                r_pM[gb].wait(tensor, c)
                last = None
                for j in range(16):
                    tok = grp * 16 + j
                    last = tensor.matmul(pM[gb][0:32, j * 32:(j + 1) * 32],
                                         qTt[:, b, :, tok],
                                         kTt[:, b, :, tok],
                                         start=True, stop=True)
                last.then_inc(pe, 1)
                c.bump("pe")
            block.tensor(t_s)
            pe_t = c.at("pe")

            def a_e(scalar):
                scalar.wait_ge(pe, pe_t)
                r_E[gb].wait(scalar, c)
                scalar.activation(out=E[:, gb, :], in_=pM[gb][0:32, :],
                                  func=AF.Exp,
                                  scale=RSQRT128).then_inc(ac, 1)
                c.bump("ac")
            block.scalar(a_e)
            r_pM[gb].used(("ac", c.at("ac")))
            ac_t = c.at("ac")

            def v_n(vector):
                vector.wait_ge(ac, ac_t)
                Ev = E.rearrange("p g (tok hp) -> p g tok hp", hp=32)[:, gb]
                vector.tensor_reduce(out=esum[:, gb, :], in_=Ev[:, :, :],
                                     op=ALU.add, axis=AX.X).then_inc(dv, 1)
                c.bump("dv")
                vector.wait_ge(dv, c.at("dv"))
                vector.reciprocal(esum[:, gb, :], esum[:, gb, :]).then_inc(dv, 1)
                c.bump("dv")
                vector.wait_ge(dv, c.at("dv"))
                r_Pn[gb].wait(vector, c)
                Pnv = Pn.rearrange("p g (tok hp) -> p g tok hp", hp=32)
                vector.tensor_tensor(out=Pnv[:, gb, :, :], in0=Ev[:, :, :],
                                     in1=bcast_inner(esum[:, gb, :], 32),
                                     op=ALU.mult).then_inc(dv, 1)
                c.bump("dv")
            block.vector(v_n)
            r_E[gb].used(("dv", c.at("dv")))
            return (v_t, c.at("dv"))

        def mix_back(grp, gb, tags):
            v_t, dv_t = tags

            def t_tr(tensor):
                tensor.wait_ge(dv, dv_t)
                r_pPT2.wait(tensor, c)
                last = None
                for j in range(16):
                    last = tensor.transpose(pPT2[0:32, j * 32:(j + 1) * 32],
                                            Pn[0:32, gb, j * 32:(j + 1) * 32],
                                            sm["identb"][0:32, 0:32])
                last.then_inc(pe, 1)
                c.bump("pe")
            block.tensor(t_tr)
            pe_t2 = c.at("pe")

            def a_tr(scalar):
                scalar.wait_ge(pe, pe_t2)
                r_sbT.wait(scalar, c)
                scalar.copy(sbPT[:, :, :], pPT2[0:32, :]).then_inc(ac, 1)
                c.bump("ac")
            block.scalar(a_tr)
            r_pPT2.used(("ac", c.at("ac")))
            r_Pn[gb].used(("ac", c.at("ac")))
            ac_t2 = c.at("ac")

            def t_c(tensor):
                tensor.wait_ge(ac, ac_t2)
                tensor.wait_ge(vsem[gb], v_t)
                r_pC2.wait(tensor, c)
                last = None
                for j in range(16):
                    last = tensor.matmul(
                        pC2[:, j * 32:(j + 1) * 32],
                        vgrp[0:32, gb, j, :],
                        sbPT[0:32, j, :],
                        start=True, stop=True)
                last.then_inc(pe, 1)
                c.bump("pe")
            block.tensor(t_c)
            tag_c = ("pe", c.at("pe"))
            r_sbT.used(tag_c)
            r_vg[gb].used(tag_c)
            r_in[b].used(tag_c)
            pe_t3 = c.at("pe")

            def v_s3(vector):
                vector.wait_ge(pe, pe_t3)
                if grp == 0:
                    r_s3[b3].wait(vector, c)
                pv = bass.AP(pC2[:, :].tensor, pC2[:, :].offset,
                             [[512, 128], [1, 32], [32, 16]])
                vector.tensor_tensor(
                    out=s3blk[:, b3, :, grp * 16:(grp + 1) * 16],
                    in0=pv,
                    in1=h[:, :, tt * 128 + grp * 16: tt * 128 + (grp + 1) * 16],
                    op=ALU.add).then_inc(dv, 1)
                c.bump("dv")
            block.vector(v_s3)
            r_pC2.used(("dv", c.at("dv")))

        for grp in range(8):
            gb = grp % 2
            tags = mix_front(grp, gb)
            mpend.append((grp, gb, tags))
            if len(mpend) >= 2:
                mix_back(*mpend.pop(0))
        while mpend:
            mix_back(*mpend.pop(0))
        dv_blk = c.at("dv")

        def a_sq(scalar):
            scalar.wait_ge(dv, dv_blk)
            if tt > 0:
                scalar.wait_ge(pe, prev_stat[0])
            scalar.activation(out=sq3[:, 0, :, :], in_=s3blk[:, b3, :, :],
                              func=AF.Square).then_inc(ac, 1)
            c.bump("ac")
        block.scalar(a_sq)
        ac_sq = c.at("ac")

        def t_st(tensor):
            tensor.wait_ge(ac, ac_sq)
            last = None
            for hc in range(KC):
                tensor.matmul(psstat[:, tt:tt + 1],
                              s3blk[:, b3, hc, :], sm["onescol"][:, :],
                              start=(stat_first[0]), stop=False,
                              skip_group_check=True)
                stat_first[0] = False
                last = tensor.matmul(psstat[:, 16 + tt:17 + tt],
                                     sq3[:, 0, hc, :], sm["onescol"][:, :],
                                     start=False, stop=False,
                                     skip_group_check=True)
            last.then_inc(pe, 1)
            c.bump("pe")
        block.tensor(t_st)
        tag_st = ("pe", c.at("pe"))
        prev_stat[0] = c.at("pe")

        def s_st(sync):
            sync.wait_ge(dv, dv_blk)
            sync.dma_start(out=s3r[:, :, tt * 128:(tt + 1) * 128],
                           in_=s3blk[:, b3, :, :]).then_inc(s3sem[b3], 16)
            c.bump(s3names[b3], 16)
        block.sync(s_st)
        r_s3[b3].used(tag_st)
        r_s3[b3].used((s3names[b3], c.at(s3names[b3])))
    return [(n, c.at(n)) for n in s3names] + [("pe", c.at("pe"))]



def emit_dots_masks(c, sm, h, centT_sb, asg_d, cc_in_d, cc_out_d, flat_scr,
                    qB, rcolT, ready_tags=()):
    block = c.block
    pe, dv, ac, cc = (c.sem(x) for x in ("pe", "dv", "ac", "cc"))
    gd = c.sem("dt_gd")

    asg_sb = c.sb("dt_asg", [1, D], I32)
    asgf = c.sb("dt_asgf", [1, D], F32)
    cidx = c.sb("dt_cidx", [64, 1], F32)
    asgB = c.sb("dt_asgB", [64, 512], F32)
    ohT = c.sb("dt_ohT", [64, D], BF16)
    poh = c.ps("dt_poh", [64, 512], F32)

    def g0(gpsimd):
        for rt in ready_tags:
            wait_tag(gpsimd, c, rt)
        gpsimd.iota(cidx[:], base=0, pattern=[[0, 1]], channel_multiplier=1,
                    allow_small_or_imprecise_dtypes=True)
        gpsimd.dma_start(out=asg_sb[:], in_=asg_d[:, :]).then_inc(gd, 16)
        c.bump("dt_gd", 16)
    block.gpsimd(g0)
    gd_t = c.at("dt_gd")

    def v0(vector):
        vector.wait_ge(gd, gd_t)
        for rt in ready_tags:
            wait_tag(vector, c, rt)
        vector.tensor_copy(asgf[:], asg_sb[:]).then_inc(dv, 1)
        c.bump("dv")
    block.vector(v0)
    dv_t0 = c.at("dv")

    for seg in range(8):
        def t0(tensor):
            if seg == 0:
                tensor.wait_ge(dv, dv_t0)
            else:
                tensor.wait_ge(dv, c.at("dv"))
            tensor.matmul(poh[0:64, :], sm["onesrow"][0:1, 0:64],
                          asgf[0:1, seg * 512:(seg + 1) * 512],
                          start=True, stop=True).then_inc(pe, 1)
            c.bump("pe")
        block.tensor(t0)
        pe_t0 = c.at("pe")

        def v1(vector):
            vector.wait_ge(pe, pe_t0)
            vector.tensor_copy(asgB[:, :], poh[0:64, :]).then_inc(dv, 1)
            c.bump("dv")
            vector.wait_ge(dv, c.at("dv"))
            vector.tensor_scalar(out=ohT[0:64, seg * 512:(seg + 1) * 512],
                                 in0=asgB[:, :], scalar1=cidx[:, 0:1],
                                 scalar2=None, op0=ALU.is_equal).then_inc(dv, 1)
            c.bump("dv")
        block.vector(v1)
    oh_t = c.at("dv")

    ED = c.sb("dt_E", [128, 64], F32)
    dsum = c.sb("dt_ds", [128, 1], F32)
    dots = c.sb("dt_d", [128, 64], F32)
    dotb = c.sb("dt_db", [128, 64], BF16)
    dmax = c.sb("dt_dm", [128, 1], F32)
    qmAll = c.sb("dt_qm", [128, 16], F32)
    cmaxA = c.sb("dt_cm", [64, 128], F32)
    cml = c.sb("dt_cml", [64, 1], F32)
    cg = c.sb("dt_cg", [64, 1], F32)
    chitb = c.sb("dt_ch", [64, 1], BF16)
    pD = c.ps("dt_pD", [128, 64], F32)
    pDT = c.ps("dt_pDT", [64, 128], BF16)

    r_pD = Res()
    r_pDT = Res()

    for tt in range(NT):
        def t1(tensor):
            r_pD.wait(tensor, c)
            last = None
            for dc in range(KC):
                last = tensor.matmul(pD[:, :],
                                     h[:, dc, tt * 128:(tt + 1) * 128],
                                     centT_sb[:, dc, :],
                                     start=(dc == 0), stop=(dc == KC - 1),
                                     skip_group_check=True)
            last.then_inc(pe, 1)
            c.bump("pe")
        block.tensor(t1)
        pe_t1 = c.at("pe")

        def a1(scalar):
            scalar.wait_ge(pe, pe_t1)
            scalar.activation(out=ED[:, :], in_=pD[:, :], func=AF.Exp,
                              scale=10.0,
                              accum_out=dsum[:, 0:1]).then_inc(ac, 1)
            c.bump("ac")
        block.scalar(a1)
        r_pD.used(("ac", c.at("ac")))
        ac_t1 = c.at("ac")

        def v2(vector):
            vector.wait_ge(ac, ac_t1)
            vector.reciprocal(dsum[:], dsum[:]).then_inc(dv, 1)
            c.bump("dv")
            vector.wait_ge(dv, c.at("dv"))
            vector.tensor_scalar(out=dots[:, :], in0=ED[:, :],
                                 scalar1=dsum[:, 0:1], scalar2=None,
                                 op0=ALU.mult).then_inc(dv, 1)
            c.bump("dv")
            vector.wait_ge(dv, c.at("dv"))
            vector.tensor_reduce(out=dmax[:], in_=dots[:, :], op=ALU.max,
                                 axis=AX.X).then_inc(dv, 1)
            c.bump("dv")
            vector.wait_ge(dv, c.at("dv"))
            vector.tensor_scalar(out=qmAll[:, tt:tt + 1], in0=dmax[:],
                                 scalar1=0.01, scalar2=None, op0=ALU.is_gt)
            r_pDT.wait(vector, c)
            vector.tensor_copy(dotb[:, :], dots[:, :]).then_inc(dv, 1)
            c.bump("dv")
        block.vector(v2)
        dv_t2 = c.at("dv")

        def t2(tensor):
            tensor.wait_ge(dv, dv_t2)
            tensor.transpose(pDT[0:64, :], dotb[:, :],
                             sm["identb"][:]).then_inc(pe, 1)
            c.bump("pe")
        block.tensor(t2)
        pe_t2 = c.at("pe")

        def v3(vector):
            vector.wait_ge(pe, pe_t2)
            vector.wait_ge(dv, c.at("dv"))
            if tt == 0:
                vector.tensor_copy(cmaxA[:, :], pDT[0:64, :]).then_inc(dv, 1)
            else:
                vector.tensor_tensor(out=cmaxA[:, :], in0=cmaxA[:, :],
                                     in1=pDT[0:64, :],
                                     op=ALU.max).then_inc(dv, 1)
            c.bump("dv")
        block.vector(v3)
        r_pDT.used(("dv", c.at("dv")))
    dv_cm = c.at("dv")

    def v4(vector):
        vector.wait_ge(dv, dv_cm)
        vector.tensor_reduce(out=cml[:], in_=cmaxA[:, :], op=ALU.max,
                             axis=AX.X).then_inc(dv, 1)
        c.bump("dv")
    block.vector(v4)
    dv_t4 = c.at("dv")

    def g1(gpsimd):
        gpsimd.wait_ge(dv, dv_t4)
        gpsimd.dma_start(out=cc_in_d[:, :], in_=cml[:]).then_inc(gd, 16)
        c.bump("dt_gd", 16)
        gpsimd.wait_ge(gd, c.at("dt_gd"))
        gpsimd.collective_compute(
            "AllReduce", ALU.max,
            replica_groups=[[0, 1, 2, 3, 4, 5, 6, 7]],
            ins=[cc_in_d[:, :]],
            outs=[cc_out_d[:, :]],
        ).then_inc(cc, 1)
        c.bump("cc")
        gpsimd.wait_ge(cc, c.at("cc"))
        gpsimd.dma_start(out=cg[:], in_=cc_out_d[:, :]).then_inc(gd, 16)
        c.bump("dt_gd", 16)
    block.gpsimd(g1)
    gd_t1 = c.at("dt_gd")

    def v5(vector):
        vector.wait_ge(gd, gd_t1)
        vector.tensor_scalar(out=chitb[:], in0=cg[:], scalar1=0.01,
                             scalar2=None, op0=ALU.is_gt).then_inc(dv, 1)
        c.bump("dv")
    block.vector(v5)
    dv_t5 = c.at("dv")

    prm = c.ps("dt_prm", [128, 32], F32)

    def t3(tensor):
        tensor.wait_ge(dv, dv_t5)
        tensor.wait_ge(dv, oh_t)
        last = None
        for oc in range(KC):
            last = tensor.matmul(prm[:, oc:oc + 1],
                                 ohT[0:64, oc * 128:(oc + 1) * 128],
                                 chitb[0:64, 0:1],
                                 start=True, stop=True)
        last.then_inc(pe, 1)
        c.bump("pe")
    block.tensor(t3)
    pe_t3 = c.at("pe")

    def v6(vector):
        vector.wait_ge(pe, pe_t3)
        vector.tensor_copy(rcolT[:, :], prm[:, :]).then_inc(dv, 1)
        c.bump("dv")
    block.vector(v6)

    pqt = c.ps("dt_pqt", [16, 128], F32)
    qrow_sb = c.sb("dt_qrs", [16, 128], F32)
    qrow = c.sb("dt_qrow", [1, T], F32)
    pqb = c.ps("dt_pqb", [128, 512], F32)
    dv_qm = c.at("dv")

    def t4(tensor):
        tensor.wait_ge(dv, dv_qm)
        tensor.transpose(pqt[0:16, :], qmAll[:, :],
                         sm["identf"][:]).then_inc(pe, 1)
        c.bump("pe")
    block.tensor(t4)
    pe_t4 = c.at("pe")

    def v7(vector):
        vector.wait_ge(pe, pe_t4)
        vector.tensor_copy(qrow_sb[:, :], pqt[0:16, :]).then_inc(dv, 1)
        c.bump("dv")
    block.vector(v7)
    dv_t7 = c.at("dv")

    def g2(gpsimd):
        gpsimd.wait_ge(dv, dv_t7)
        gpsimd.dma_start(out=flat_scr[0:16, 0:128],
                         in_=qrow_sb[:, :]).then_inc(gd, 16)
        c.bump("dt_gd", 16)
        gpsimd.wait_ge(gd, c.at("dt_gd"))
        fsv = flat_scr.rearrange("p (a f) -> a p f", a=2)[0:1, :, :]
        qrv = qrow.rearrange("a (p f) -> a p f", p=16)
        gpsimd.dma_start(out=qrv[:, :, :], in_=fsv).then_inc(gd, 16)
        c.bump("dt_gd", 16)
    block.gpsimd(g2)
    gd_t2 = c.at("dt_gd")

    for seg in range(NSEG):
        def t5(tensor):
            if seg == 0:
                tensor.wait_ge(gd, gd_t2)
            else:
                tensor.wait_ge(dv, c.at("dv"))
            tensor.matmul(pqb[:, :], sm["onesrow"][:, :],
                          qrow[0:1, seg * 512:(seg + 1) * 512],
                          start=True, stop=True).then_inc(pe, 1)
            c.bump("pe")
        block.tensor(t5)
        pe_t5 = c.at("pe")

        def v8(vector):
            vector.wait_ge(pe, pe_t5)
            vector.tensor_copy(qB[:, seg * 512:(seg + 1) * 512],
                               pqb[:, :]).then_inc(dv, 1)
            c.bump("dv")
        block.vector(v8)
    return ("dv", c.at("dv"))


def emit_load_chunkedT(c, dst, src_dram, name, ready_tags=()):
    block = c.block
    gd = c.sem(name)
    srcr = src_dram.rearrange("(kc p) n -> p kc n", p=128)

    def g(gpsimd):
        for rt in ready_tags:
            wait_tag(gpsimd, c, rt)
        gpsimd.dma_start(out=dst[:], in_=srcr[:, :, :]).then_inc(gd, 16)
        c.bump(name, 16)
    block.gpsimd(g)
    return (name, c.at(name))


def finish(c):
    block = c.block

    def s(sync):
        for name in list(c.n):
            sync.wait_ge(c.sem(name), c.n[name])
    block.sync(s)



def build_l0attn():
    nc = bass.Bass("TRN2", target_bir_lowering=False)
    xT = nc.dram_tensor("xT", [D, T], F32, kind="ExternalInput").ap()
    WqT = nc.dram_tensor("WqT", [D, 512], F32, kind="ExternalInput").ap()
    WkT = nc.dram_tensor("WkT", [D, 512], F32, kind="ExternalInput").ap()
    WvT = nc.dram_tensor("WvT", [D, 512], F32, kind="ExternalInput").ap()
    bq = nc.dram_tensor("bq", [1, 512], F32, kind="ExternalInput").ap()
    bk = nc.dram_tensor("bk", [1, 512], F32, kind="ExternalInput").ap()
    bv = nc.dram_tensor("bv", [1, 512], F32, kind="ExternalInput").ap()
    WoT = nc.dram_tensor("WoT", [512, D], F32, kind="ExternalInput").ap()
    bo = nc.dram_tensor("bo", [1, D], F32, kind="ExternalInput").ap()
    h1T = nc.dram_tensor("h1T", [D, T], BF16, kind="ExternalOutput").ap()

    scr_q = nc.dram_tensor("scr_q", [512, T], BF16, kind="Internal").ap()
    scr_k = nc.dram_tensor("scr_k", [512, T], BF16, kind="Internal").ap()
    scr_v = nc.dram_tensor("scr_v", [T, 512], BF16, kind="Internal").ap()
    s_scr = nc.dram_tensor("s_scr", [D, T], BF16, kind="Internal").ap()
    flat_scr = nc.dram_tensor("flat_scr", [16, 256], F32, kind="Internal").ap()

    with ExitStack() as S:
        block = S.enter_context(nc.Block())
        c = Ctx(nc, S, block)
        sm = emit_smalls(c)
        h = c.sb("h", [128, KC, T], BF16)
        htag = emit_load_stream(c, h, xT, cast=True)

        bqT, t1 = emit_bias_T(c, sm, bq, 4, "bq")
        bkT, t2 = emit_bias_T(c, sm, bk, 4, "bk")
        bvT, t3 = emit_bias_T(c, sm, bv, 4, "bv")
        boT, t4 = emit_bias_T(c, sm, bo, 32, "bo")

        sink_tags = {}
        prev_stores = ()
        for (w_d, bT, tg, mksink) in (
                (WqT, bqT, "dq", lambda cc: QKSink(cc, "qs", bqT, scr_q)),
                (WkT, bkT, "dk", lambda cc: QKSink(cc, "ks", bkT, scr_k)),
                (WvT, bvT, "dv_", lambda cc: VSink(cc, sm, "vs", bvT, scr_v))):
            with ExitStack() as S1:
                c.S = S1
                prev = (("pe", c.at("pe")), ("dv", c.at("dv")),
                        ("ac", c.at("ac"))) + prev_stores
                sink = mksink(c)
                emit_dense(c, WT_dram=w_d, kchunks=KC, ochunks=4, moving=h,
                           sink=sink, tag=tg,
                           ready_tags=(htag, t1, t2, t3, t4) + prev)
                sink_tags[tg] = sink.store_tags(c)
                prev_stores = tuple(sink_tags[tg])
                c.S = S
        qsink_tags = sink_tags["dq"]
        ksink_tags = sink_tags["dk"]
        vsink_tags = sink_tags["dv_"]

        a1_tags = tuple(qsink_tags + ksink_tags + vsink_tags) + (
            ("pe", c.at("pe")), ("ac", c.at("ac")), ("dv", c.at("dv")))
        ctxT = c.sb("ctxT", [128, 4, T], BF16)
        with ExitStack() as S2:
            c.S = S2
            atag = emit_l0_attention(c, sm, scr_q, scr_k, scr_v, ctxT,
                                     ready_tags=a1_tags)
            c.S = S

        psstat0 = c.ps("ln0_st", [128, 32], F32)
        with ExitStack() as S3:
            c.S = S3
            lnsink = LNSink(c, sm, "ln0", h, boT, s_scr, psstat0)
            emit_dense(c, WT_dram=WoT, kchunks=4, ochunks=KC, moving=ctxT,
                       sink=lnsink, tag="do", ready_tags=(atag, t4))
            ln_tags = lnsink.store_tags(c)
            c.S = S
        with ExitStack() as S4:
            c.S = S4
            emit_normalize(c, sm, "n0", s_scr, h, psstat0, flat_scr,
                           dest_mode="both", out_dram=h1T,
                           store_tags=ln_tags)
            c.S = S
        finish(c)
    return nc


def build_l0ffn():
    nc = bass.Bass("TRN2", target_bir_lowering=False)
    h1T_d = nc.dram_tensor("h1T", [D, T], BF16, kind="ExternalInput").ap()
    WbT = nc.dram_tensor("WbT", [D, D], F32, kind="ExternalInput").ap()
    bb = nc.dram_tensor("bb", [1, D], F32, kind="ExternalInput").ap()
    encT = nc.dram_tensor("encT", [D, 40], F32, kind="ExternalInput").ap()
    coefs0 = nc.dram_tensor("coefs0", [3, D, 32], F32, kind="ExternalInput").ap()
    coefs1 = nc.dram_tensor("coefs1", [3, D, 8], F32, kind="ExternalInput").ap()
    mask0 = nc.dram_tensor("mask0", [3, D, 32], U8, kind="ExternalInput").ap()
    mask1 = nc.dram_tensor("mask1", [3, D, 8], U8, kind="ExternalInput").ap()
    Bc = nc.dram_tensor("Bc", [D, D], F32, kind="ExternalInput").ap()
    h2T = nc.dram_tensor("h2T", [D, T], BF16, kind="ExternalOutput").ap()

    s_scr = nc.dram_tensor("s_scr", [D, T], BF16, kind="Internal").ap()
    flat_scr = nc.dram_tensor("flat_scr", [16, 256], F32, kind="Internal").ap()

    with ExitStack() as S:
        block = S.enter_context(nc.Block())
        c = Ctx(nc, S, block)
        sm = emit_smalls(c)
        GT = c.sb("GT", [40, D], BF16)
        eT = c.sb("eT", [40, T], BF16)
        encT_sb = c.sb("encT_sb", [128, KC, 40], BF16)
        h = c.sb("h", [128, KC, T], BF16)

        etag = emit_load_chunkedT(c, encT_sb, encT, "enc_d")
        htag = emit_load_stream(c, h, h1T_d, cast=False)
        bbT, t1 = emit_bias_T(c, sm, bb, 32, "bb")

        with ExitStack() as S1:
            c.S = S1
            gtag = emit_G_setup(c, sm, coefs0, mask0, coefs1, mask1, Bc, GT,
                                ready_tags=(sm["ready"], t1))
            c.S = S

        psstat1 = c.ps("ln1_st", [128, 32], F32)
        with ExitStack() as S2:
            c.S = S2
            with ExitStack() as S2e:
                c.S = S2e
                ettag = emit_eT(c, sm, encT_sb, h, eT,
                                ready_tags=(etag, htag, gtag))
                c.S = S2

            def upd(tensor, oc, tseg, psum):
                return tensor.matmul(
                    psum[:, :], GT[0:40, oc * 128:(oc + 1) * 128],
                    eT[0:40, tseg * 512:(tseg + 1) * 512],
                    start=False, stop=True, skip_group_check=True)

            lnsink = LNSink(c, sm, "ln1", h, bbT, s_scr, psstat1)
            emit_dense(c, WT_dram=WbT, kchunks=KC, ochunks=KC, moving=h,
                       sink=lnsink, extra_mm=upd, tag="db",
                       ready_tags=(gtag, ettag, t1))
            ln_tags = lnsink.store_tags(c)
            c.S = S
        with ExitStack() as S2n:
            c.S = S2n
            emit_normalize(c, sm, "n1", s_scr, h, psstat1, flat_scr,
                           dest_mode="both", out_dram=h2T,
                           store_tags=ln_tags)
            c.S = S
        finish(c)
    return nc


def build_l1qkv():
    nc = bass.Bass("TRN2", target_bir_lowering=False)
    h2T_d = nc.dram_tensor("h2T", [D, T], BF16, kind="ExternalInput").ap()
    wqT = nc.dram_tensor("wqT", [D, D], F32, kind="ExternalInput").ap()
    wkT = nc.dram_tensor("wkT", [D, D], F32, kind="ExternalInput").ap()
    wvT = nc.dram_tensor("wvT", [D, D], F32, kind="ExternalInput").ap()
    bq = nc.dram_tensor("bq", [1, D], F32, kind="ExternalInput").ap()
    bk = nc.dram_tensor("bk", [1, D], F32, kind="ExternalInput").ap()
    bv = nc.dram_tensor("bv", [1, D], F32, kind="ExternalInput").ap()
    q1 = nc.dram_tensor("q1", [D, T], BF16, kind="ExternalOutput").ap()
    k1 = nc.dram_tensor("k1", [D, T], BF16, kind="ExternalOutput").ap()
    v1 = nc.dram_tensor("v1", [T, D], BF16, kind="ExternalOutput").ap()

    with ExitStack() as S:
        block = S.enter_context(nc.Block())
        c = Ctx(nc, S, block)
        sm = emit_smalls(c)
        h = c.sb("h", [128, KC, T], BF16)
        htag = emit_load_stream(c, h, h2T_d, cast=False)
        bqT, t1 = emit_bias_T(c, sm, bq, 32, "bq")
        bkT, t2 = emit_bias_T(c, sm, bk, 32, "bk")
        bvT, t3 = emit_bias_T(c, sm, bv, 32, "bv")

        prev_stores = ()
        for (w_d, bT, out_d, tg, rt) in ((wqT, bqT, q1, "pq", t1),
                                         (wkT, bkT, k1, "pk", t2),
                                         (wvT, bvT, v1, "pv", t3)):
            with ExitStack() as S1:
                c.S = S1
                prev = (("pe", c.at("pe")), ("dv", c.at("dv")),
                        ("ac", c.at("ac"))) + prev_stores
                if tg == "pv":
                    sink = VSink(c, sm, tg + "s", bT, out_d)
                else:
                    sink = QKSink(c, tg + "s", bT, out_d)
                emit_dense(c, WT_dram=w_d, kchunks=KC, ochunks=KC, moving=h,
                           sink=sink, tag=tg,
                           ready_tags=(htag, t1, t2, t3) + prev)
                prev_stores = tuple(sink.store_tags(c))
                c.S = S
        finish(c)
    return nc


def build_l1mix():
    nc = bass.Bass("TRN2", target_bir_lowering=False)
    h2T_d = nc.dram_tensor("h2T", [D, T], BF16, kind="ExternalInput").ap()
    q1 = nc.dram_tensor("q1", [D, T], BF16, kind="ExternalInput").ap()
    k1 = nc.dram_tensor("k1", [D, T], BF16, kind="ExternalInput").ap()
    v1 = nc.dram_tensor("v1", [T, D], BF16, kind="ExternalInput").ap()
    h3T = nc.dram_tensor("h3T", [D, T], BF16, kind="ExternalOutput").ap()
    s_scr = nc.dram_tensor("s_scr", [D, T], BF16, kind="Internal").ap()
    flat_scr = nc.dram_tensor("flat_scr", [16, 256], F32, kind="Internal").ap()

    with ExitStack() as S:
        block = S.enter_context(nc.Block())
        c = Ctx(nc, S, block)
        sm = emit_smalls(c)
        h = c.sb("h", [128, KC, T], BF16)
        htag = emit_load_stream(c, h, h2T_d, cast=False)
        psstat = c.ps("mx_st", [128, 32], F32)
        with ExitStack() as S1:
            c.S = S1
            mix_tags = emit_l1mix(c, sm, q1, k1, v1, h, s_scr, psstat,
                                  ready_tags=(htag, sm["ready"]))
            c.S = S
        with ExitStack() as S2:
            c.S = S2
            emit_normalize(c, sm, "n2", s_scr, h, psstat, flat_scr,
                           dest_mode="both", out_dram=h3T,
                           store_tags=mix_tags)
            c.S = S
        finish(c)
    return nc


def build_l1ffn():
    nc = bass.Bass("TRN2", target_bir_lowering=False)
    h3T_d = nc.dram_tensor("h3T", [D, T], BF16, kind="ExternalInput").ap()
    WdT = nc.dram_tensor("WdT", [D, D], F32, kind="ExternalInput").ap()
    bd = nc.dram_tensor("bd", [1, D], F32, kind="ExternalInput").ap()
    centT = nc.dram_tensor("centT", [D, 64], F32, kind="ExternalInput").ap()
    asg = nc.dram_tensor("asg", [1, D], I32, kind="ExternalInput").ap()
    outT = nc.dram_tensor("outT", [D, T], F32, kind="ExternalOutput").ap()

    s_scr = nc.dram_tensor("s_scr", [D, T], BF16, kind="Internal").ap()
    flat_scr = nc.dram_tensor("flat_scr", [16, 256], F32, kind="Internal").ap()
    cc_in = nc.dram_tensor("cc_in", [64, 1], F32, kind="Internal").ap()
    cc_out = nc.dram_tensor("cc_out", [64, 1], F32, kind="Internal").ap()

    with ExitStack() as S:
        block = S.enter_context(nc.Block())
        c = Ctx(nc, S, block)
        sm = emit_smalls(c)
        h = c.sb("h", [128, KC, T], BF16)
        htag = emit_load_stream(c, h, h3T_d, cast=False)
        centT_sb = c.sb("centT_sb", [128, KC, 64], BF16)
        ctag = emit_load_chunkedT(c, centT_sb, centT, "cen_d")
        bdT, t1 = emit_bias_T(c, sm, bd, 32, "bd")
        qB = c.sb("qB", [128, T], BF16)
        rcolT = c.sb("rcolT", [128, KC], F32)

        with ExitStack() as S1:
            c.S = S1
            mtag = emit_dots_masks(c, sm, h, centT_sb, asg, cc_in, cc_out,
                                   flat_scr, qB, rcolT,
                                   ready_tags=(htag, ctag, t1, sm["ready"]))
            c.S = S

        psstat3 = c.ps("ln3_st", [128, 32], F32)
        with ExitStack() as S2:
            c.S = S2
            lnsink = LNSink(c, sm, "ln3", h, bdT, s_scr, psstat3,
                            mask=(rcolT, qB))
            emit_dense(c, WT_dram=WdT, kchunks=KC, ochunks=KC, moving=h,
                       sink=lnsink, tag="dd", ready_tags=(mtag, t1))
            ln_tags = lnsink.store_tags(c)
            c.S = S
        with ExitStack() as S2n:
            c.S = S2n
            emit_normalize(c, sm, "n3", s_scr, h, psstat3, flat_scr,
                           dest_mode="out", out_dram=outT,
                           store_tags=ln_tags)
            c.S = S
        finish(c)
    return nc



def _np(x):
    return np.asarray(x)


def _bf(x):
    return np.ascontiguousarray(x)


def prep_host(inputs):
    P = {}
    x = _np(inputs["L0_attn_x"]).astype(np.float32, copy=False)
    P["xT"] = [np.ascontiguousarray(x[b].T) for b in range(8)]
    Wqk = _np(inputs["L0_attn_Wqk_w"])
    P["WqT"] = np.ascontiguousarray(Wqk[0::2].T)
    P["WkT"] = np.ascontiguousarray(Wqk[1::2].T)
    bqk = _np(inputs["L0_attn_Wqk_b"])
    P["bq0"] = np.ascontiguousarray(bqk[0::2].reshape(1, 512))
    P["bk0"] = np.ascontiguousarray(bqk[1::2].reshape(1, 512))
    P["WvT"] = np.ascontiguousarray(_np(inputs["L0_attn_Wv_w"]).T)
    P["bv0"] = _np(inputs["L0_attn_Wv_b"]).reshape(1, 512)
    P["WoT"] = np.ascontiguousarray(_np(inputs["L0_attn_Wo_w"]).T)
    P["bo0"] = _np(inputs["L0_attn_Wo_b"]).reshape(1, D)

    P["WbT"] = np.ascontiguousarray(_np(inputs["L0_ffn_W_base"]).T)
    P["bb"] = _np(inputs["L0_ffn_b_base"]).reshape(1, D)
    enc = np.concatenate([_np(inputs["L0_ffn_enc0"]),
                          _np(inputs["L0_ffn_enc1"])], axis=0)
    P["encT"] = np.ascontiguousarray(enc.T)
    P["coefs0"] = np.ascontiguousarray(_np(inputs["L0_ffn_coefs0"]))
    P["coefs1"] = np.ascontiguousarray(_np(inputs["L0_ffn_coefs1"]))
    P["mask0"] = np.ascontiguousarray(_np(inputs["L0_ffn_mask0"])).view(np.uint8)
    P["mask1"] = np.ascontiguousarray(_np(inputs["L0_ffn_mask1"])).view(np.uint8)
    P["Bc"] = fourier_basis_f32()

    P["wqT"] = np.ascontiguousarray(_np(inputs["L1_attn_wq"]).T)
    P["wkT"] = np.ascontiguousarray(_np(inputs["L1_attn_wk"]).T)
    P["wvT"] = np.ascontiguousarray(_np(inputs["L1_attn_wv"]).T)
    P["bq1"] = _np(inputs["L1_attn_bq"]).reshape(1, D)
    P["bk1"] = _np(inputs["L1_attn_bk"]).reshape(1, D)
    P["bv1"] = _np(inputs["L1_attn_bv"]).reshape(1, D)

    P["WdT"] = np.ascontiguousarray(_np(inputs["L1_ffn_weight"]).T)
    P["bd"] = _np(inputs["L1_ffn_bias"]).reshape(1, D)
    P["centT"] = np.ascontiguousarray(_np(inputs["L1_ffn_centroids"]).T)
    P["asg"] = _np(inputs["L1_ffn_assignments"]).astype(np.int32).reshape(1, D)
    return P


def kernel5(**inputs):
    P = prep_host(inputs)
    cores = list(range(8))

    nc1 = build_l0attn()
    r1 = run_bass_kernel_spmd(nc1, [dict(
        xT=P["xT"][b], WqT=P["WqT"], WkT=P["WkT"], WvT=P["WvT"],
        bq=P["bq0"], bk=P["bk0"], bv=P["bv0"], WoT=P["WoT"], bo=P["bo0"],
    ) for b in range(8)], core_ids=cores)
    h1 = [r1.results[b]["h1T"] for b in range(8)]

    nc2 = build_l0ffn()
    r2 = run_bass_kernel_spmd(nc2, [dict(
        h1T=h1[b], WbT=P["WbT"], bb=P["bb"], encT=P["encT"],
        coefs0=P["coefs0"], coefs1=P["coefs1"], mask0=P["mask0"],
        mask1=P["mask1"], Bc=P["Bc"],
    ) for b in range(8)], core_ids=cores)
    h2 = [r2.results[b]["h2T"] for b in range(8)]

    nc3 = build_l1qkv()
    r3 = run_bass_kernel_spmd(nc3, [dict(
        h2T=h2[b], wqT=P["wqT"], wkT=P["wkT"], wvT=P["wvT"],
        bq=P["bq1"], bk=P["bk1"], bv=P["bv1"],
    ) for b in range(8)], core_ids=cores)

    nc4 = build_l1mix()
    r4 = run_bass_kernel_spmd(nc4, [dict(
        h2T=h2[b], q1=r3.results[b]["q1"], k1=r3.results[b]["k1"],
        v1=r3.results[b]["v1"],
    ) for b in range(8)], core_ids=cores)
    h3 = [r4.results[b]["h3T"] for b in range(8)]

    nc5 = build_l1ffn()
    r5 = run_bass_kernel_spmd(nc5, [dict(
        h3T=h3[b], WdT=P["WdT"], bd=P["bd"], centT=P["centT"], asg=P["asg"],
    ) for b in range(8)], core_ids=cores)

    out = np.empty((8, T, D), np.float32)
    for b in range(8):
        out[b] = np.asarray(r5.results[b]["outT"]).T
    return out


def barrier(c, *extra):
    tags = [(n, c.n[n]) for n in c.n if c.n[n] > 0 and n != "cc"]
    for e in extra:
        tags.extend(e)
    return tuple(tags)


def build_fused(debug=False, skip=()):
    nc = bass.Bass("TRN2", target_bir_lowering=False)
    AP_ = lambda n, s, d, k: nc.dram_tensor(n, s, d, kind=k).ap()
    xT = AP_("xT", [D, T], F32, "ExternalInput")
    WqT = AP_("WqT", [D, 512], F32, "ExternalInput")
    WkT = AP_("WkT", [D, 512], F32, "ExternalInput")
    WvT = AP_("WvT", [D, 512], F32, "ExternalInput")
    bq0 = AP_("bq0", [1, 512], F32, "ExternalInput")
    bk0 = AP_("bk0", [1, 512], F32, "ExternalInput")
    bv0 = AP_("bv0", [1, 512], F32, "ExternalInput")
    WoT = AP_("WoT", [512, D], F32, "ExternalInput")
    bo0 = AP_("bo0", [1, D], F32, "ExternalInput")
    WbT = AP_("WbT", [D, D], F32, "ExternalInput")
    bb = AP_("bb", [1, D], F32, "ExternalInput")
    encT = AP_("encT", [D, 40], F32, "ExternalInput")
    coefs0 = AP_("coefs0", [3, D, 32], F32, "ExternalInput")
    coefs1 = AP_("coefs1", [3, D, 8], F32, "ExternalInput")
    mask0 = AP_("mask0", [3, D, 32], U8, "ExternalInput")
    mask1 = AP_("mask1", [3, D, 8], U8, "ExternalInput")
    Bc = AP_("Bc", [D, D], F32, "ExternalInput")
    wqT = AP_("wqT", [D, D], F32, "ExternalInput")
    wkT = AP_("wkT", [D, D], F32, "ExternalInput")
    wvT = AP_("wvT", [D, D], F32, "ExternalInput")
    bq1 = AP_("bq1", [1, D], F32, "ExternalInput")
    bk1 = AP_("bk1", [1, D], F32, "ExternalInput")
    bv1 = AP_("bv1", [1, D], F32, "ExternalInput")
    WdT = AP_("WdT", [D, D], F32, "ExternalInput")
    bd = AP_("bd", [1, D], F32, "ExternalInput")
    centT = AP_("centT", [D, 64], F32, "ExternalInput")
    asg = AP_("asg", [1, D], I32, "ExternalInput")
    outT = AP_("outT", [D, T], F32, "ExternalOutput")

    scr_q = AP_("scr_q", [512, T], BF16, "Internal")
    scr_k = AP_("scr_k", [512, T], BF16, "Internal")
    scr_v = AP_("scr_v", [T, 512], BF16, "Internal")
    scr_q1 = AP_("scr_q1", [D, T], BF16, "Internal")
    scr_k1 = AP_("scr_k1", [D, T], BF16, "Internal")
    scr_v1 = AP_("scr_v1", [T, D], BF16, "Internal")
    s_scr = AP_("s_scr", [D, T], BF16, "Internal")
    flat0 = AP_("flat0", [16, 256], F32, "Internal")
    flat1 = AP_("flat1", [16, 256], F32, "Internal")
    flat2 = AP_("flat2", [16, 256], F32, "Internal")
    flat3 = AP_("flat3", [16, 256], F32, "Internal")
    cc_in = AP_("cc_in", [64, 1], F32, "Internal")
    cc_out = AP_("cc_out", [64, 1], F32, "Internal")

    dbgs = {}
    if debug:
        for nm in ("dbg0", "dbg1", "dbg2"):
            dbgs[nm] = AP_(nm, [D, T], BF16, "ExternalOutput")

    with ExitStack() as S:
        block = S.enter_context(nc.Block())
        c = Ctx(nc, S, block)
        sm = emit_smalls(c)
        h = c.sb("h", [128, KC, T], BF16)
        GT = c.sb("GT", [40, D], BF16)

        def dump_h(nm, tag):
            if not debug:
                return
            dd = dbgs[nm].rearrange("(kc p) t -> p kc t", p=128)
            sem = c.sem(nm + "_d")

            def s_(sync):
                wait_tag(sync, c, tag)
                sync.dma_start(out=dd[:, :, :], in_=h[:]).then_inc(sem, 16)
                c.bump(nm + "_d", 16)
            block.sync(s_)

        with ExitStack() as S0:
            c.S = S0
            gtag = emit_G_setup(c, sm, coefs0, mask0, coefs1, mask1, Bc, GT,
                                ready_tags=(sm["ready"],))
            c.S = S
        htag = emit_load_stream(c, h, xT, cast=True)

        bqT, t1 = emit_bias_T(c, sm, bq0, 4, "bq")
        bkT, t2 = emit_bias_T(c, sm, bk0, 4, "bk")
        bvT, t3 = emit_bias_T(c, sm, bv0, 4, "bv")
        boT, t4 = emit_bias_T(c, sm, bo0, 32, "bo")
        bias_tags = (t1, t2, t3, t4, gtag)

        sink_tags = {}
        prev_stores = ()
        for (w_d, tg, mksink) in (
                (WqT, "dq", lambda cc: QKSink(cc, "qs", bqT, scr_q)),
                (WkT, "dk", lambda cc: QKSink(cc, "ks", bkT, scr_k)),
                (WvT, "dv_", lambda cc: VSink(cc, sm, "vs", bvT, scr_v))):
            with ExitStack() as S1:
                c.S = S1
                prev = barrier(c) + prev_stores
                sink = mksink(c)
                emit_dense(c, WT_dram=w_d, kchunks=KC, ochunks=4, moving=h,
                           sink=sink, tag=tg,
                           ready_tags=(htag,) + bias_tags + prev)
                sink_tags[tg] = sink.store_tags(c)
                prev_stores = tuple(sink_tags[tg])
                c.S = S
        a1_tags = barrier(c, sink_tags["dq"], sink_tags["dk"],
                          sink_tags["dv_"])

        SL0 = ExitStack()
        c.S = SL0
        psstat0 = c.ps("ln0_st", [128, 32], F32)
        with ExitStack() as SA:
            c.S = SA
            ctxT = c.sb("ctxT", [128, 4, T], BF16)
            with ExitStack() as S2:
                c.S = S2
                atag = emit_l0_attention(c, sm, scr_q, scr_k, scr_v, ctxT,
                                         ready_tags=a1_tags)
                c.S = SA
            with ExitStack() as S3:
                c.S = S3
                lnsink = LNSink(c, sm, "ln0", h, boT, s_scr, psstat0)
                emit_dense(c, WT_dram=WoT, kchunks=4, ochunks=KC, moving=ctxT,
                           sink=lnsink, tag="do",
                           ready_tags=(atag,) + barrier(c))
                ln_tags = lnsink.store_tags(c)
                c.S = SA
            c.S = SL0
        with ExitStack() as S4:
            c.S = S4
            n0tag = emit_normalize(c, sm, "n0", s_scr, h, psstat0, flat0,
                                   dest_mode="h", store_tags=ln_tags)
            dump_h("dbg0", n0tag)
            c.S = SL0
        SL0.close()
        c.S = S

        with ExitStack() as SF:
            c.S = SF
            encT_sb = c.sb("encT_sb", [128, KC, 40], BF16)
            eT = c.sb("eT", [40, T], BF16)
            bar0 = barrier(c)
            etag = emit_load_chunkedT(c, encT_sb, encT, "enc_d",
                                      ready_tags=bar0)
            bbT, tb1 = emit_bias_T(c, sm, bb, 32, "bb", ready_tags=bar0)
            psstat1 = c.ps("ln1_st", [128, 32], F32)
            with ExitStack() as SFe:
                c.S = SFe
                if "eT" in skip:
                    gpx = c.sem("gp")
                    def _z(gpsimd):
                        wait_tag(gpsimd, c, n0tag)
                        gpsimd.memset(eT[0:40, :], 0.0).then_inc(gpx, 1)
                        c.bump("gp")
                    c.block.gpsimd(_z)
                    ettag = ("gp", c.at("gp"))
                else:
                    ettag = emit_eT(c, sm, encT_sb, h, eT,
                                    ready_tags=(etag, n0tag) + barrier(c))
                c.S = SF

            def upd(tensor, oc, tseg, psum):
                return tensor.matmul(
                    psum[:, :], GT[0:40, oc * 128:(oc + 1) * 128],
                    eT[0:40, tseg * 512:(tseg + 1) * 512],
                    start=False, stop=True, skip_group_check=True)

            with ExitStack() as SFd:
                c.S = SFd
                lnsink1 = LNSink(c, sm, "ln1", h, bbT, s_scr, psstat1)
                emit_dense(c, WT_dram=WbT, kchunks=KC, ochunks=KC, moving=h,
                           sink=lnsink1,
                           extra_mm=(None if "upd" in skip else upd),
                           tag="db",
                           ready_tags=(ettag, tb1) + barrier(c))
                ln1_tags = lnsink1.store_tags(c)
                c.S = SF
            with ExitStack() as SFn:
                c.S = SFn
                n1tag = emit_normalize(c, sm, "n1", s_scr, h, psstat1, flat1,
                                       dest_mode="h", store_tags=ln1_tags)
                dump_h("dbg1", n1tag)
                c.S = SF
            c.S = S

        prev_stores = ()
        qkv_tags = {}
        with ExitStack() as SQ:
            c.S = SQ
            barq = barrier(c)
            bq1T, tq1 = emit_bias_T(c, sm, bq1, 32, "b1q", ready_tags=barq)
            bk1T, tq2 = emit_bias_T(c, sm, bk1, 32, "b1k")
            bv1T, tq3 = emit_bias_T(c, sm, bv1, 32, "b1v")
            for (w_d, bT, out_d, tg) in ((wqT, bq1T, scr_q1, "pq"),
                                         (wkT, bk1T, scr_k1, "pk"),
                                         (wvT, bv1T, scr_v1, "pv")):
                with ExitStack() as S1:
                    c.S = S1
                    prev = barrier(c) + prev_stores
                    if tg == "pv":
                        sink = VSink(c, sm, tg + "s", bT, out_d)
                    else:
                        sink = QKSink(c, tg + "s", bT, out_d)
                    emit_dense(c, WT_dram=w_d, kchunks=KC, ochunks=KC,
                               moving=h, sink=sink, tag=tg,
                               ready_tags=(n1tag, tq1, tq2, tq3) + prev)
                    qkv_tags[tg] = sink.store_tags(c)
                    prev_stores = tuple(qkv_tags[tg])
                    c.S = S
            c.S = S

        SL2 = ExitStack()
        c.S = SL2
        psstat2 = c.ps("ln2_st", [128, 32], F32)
        with ExitStack() as SM:
            c.S = SM
            mix_tags = emit_l1mix(
                c, sm, scr_q1, scr_k1, scr_v1, h, s_scr, psstat2,
                ready_tags=barrier(c, qkv_tags["pq"], qkv_tags["pk"],
                                   qkv_tags["pv"]))
            c.S = SL2
        with ExitStack() as SMn:
            c.S = SMn
            n2tag = emit_normalize(c, sm, "n2", s_scr, h, psstat2, flat2,
                                   dest_mode="h", store_tags=mix_tags)
            dump_h("dbg2", n2tag)
            c.S = SL2
        SL2.close()
        c.S = S

        with ExitStack() as SD:
            c.S = SD
            centT_sb = c.sb("centT_sb", [128, KC, 64], BF16)
            bard = barrier(c)
            ctag = emit_load_chunkedT(c, centT_sb, centT, "cen_d",
                                      ready_tags=bard)
            bdT, td1 = emit_bias_T(c, sm, bd, 32, "bd", ready_tags=bard)
            qB = c.sb("qB", [128, T], BF16)
            rcolT = c.sb("rcolT", [128, KC], F32)
            psstat3 = c.ps("ln3_st", [128, 32], F32)
            with ExitStack() as SD1:
                c.S = SD1
                mtag = emit_dots_masks(
                    c, sm, h, centT_sb, asg, cc_in, cc_out, flat3, qB, rcolT,
                    ready_tags=(n2tag, ctag, td1) + barrier(c))
                c.S = SD
            with ExitStack() as SD2:
                c.S = SD2
                lnsink3 = LNSink(c, sm, "ln3", h, bdT, s_scr, psstat3,
                                 mask=(rcolT, qB))
                emit_dense(c, WT_dram=WdT, kchunks=KC, ochunks=KC, moving=h,
                           sink=lnsink3, tag="dd",
                           ready_tags=(mtag,) + barrier(c))
                ln3_tags = lnsink3.store_tags(c)
                c.S = SD
            with ExitStack() as SD3:
                c.S = SD3
                emit_normalize(c, sm, "n3", s_scr, h, psstat3, flat3,
                               dest_mode="out", out_dram=outT,
                               store_tags=ln3_tags)
                c.S = SD
            c.S = S
        finish(c)
    return nc


def kernel(**inputs):
    P = prep_host(inputs)
    cores = list(range(8))
    nc = build_fused()
    r = run_bass_kernel_spmd(nc, [dict(
        xT=P["xT"][b], WqT=P["WqT"], WkT=P["WkT"], WvT=P["WvT"],
        bq0=P["bq0"], bk0=P["bk0"], bv0=P["bv0"], WoT=P["WoT"], bo0=P["bo0"],
        WbT=P["WbT"], bb=P["bb"], encT=P["encT"], coefs0=P["coefs0"],
        coefs1=P["coefs1"], mask0=P["mask0"], mask1=P["mask1"], Bc=P["Bc"],
        wqT=P["wqT"], wkT=P["wkT"], wvT=P["wvT"],
        bq1=P["bq1"], bk1=P["bk1"], bv1=P["bv1"],
        WdT=P["WdT"], bd=P["bd"], centT=P["centT"], asg=P["asg"],
    ) for b in range(8)], core_ids=cores)
    out = np.empty((8, T, D), np.float32)
    for b in range(8):
        out[b] = np.asarray(r.results[b]["outT"]).T
    return out
